# version 19
# baseline (speedup 1.0000x reference)
import math
import numpy as np
from contextlib import ExitStack
from ml_dtypes import bfloat16

import concourse.bass as bass
import concourse.mybir as mybir
from concourse.bass_utils import run_bass_kernel_spmd

F32 = mybir.dt.float32
BF16 = mybir.dt.bfloat16
AF = mybir.ActivationFunctionType
ALU = mybir.AluOpType

N_CORES = 8
T = 8192
D = 4096
HD = 128
NQH = 32
NKVH = 8
TPC = 1024
NSLOT = 8
DCH = 32
NCH = 64
SCALE = 1.0 / math.sqrt(HD)
ESHIFT = -15.0
LN_EPS = 1e-5

SWAP_MASK = []
for _i in range(0, 32, 2):
    SWAP_MASK += [_i + 1, _i]


class Sem:
    def __init__(self, nc, es, name):
        self.h = es.enter_context(nc.semaphore(name))
        self.n = 0

    def inc(self, instr, k=1):
        instr.then_inc(self.h, k)
        self.n += k
        return self.n


HEAD_AT = list(range(32))


def build(taps=()):
    taps = set(taps)
    nc = bass.Bass("TRN2", target_bir_lowering=False)

    x_fm = nc.dram_tensor("x_fm", [D, TPC], F32, kind="ExternalInput")
    x_tm = nc.dram_tensor("x_tm", [NCH, 128, 512], F32, kind="ExternalInput")
    wq = nc.dram_tensor("wq", [NQH, 128, DCH, 128], F32, kind="ExternalInput")
    wk = nc.dram_tensor("wk", [NKVH, 128, DCH, 128], F32, kind="ExternalInput")
    wv = nc.dram_tensor("wv", [NKVH, 128, DCH, 128], F32, kind="ExternalInput")
    wt = nc.dram_tensor("wt", [8, 128, DCH, 512], F32, kind="ExternalInput")
    wo = nc.dram_tensor("wo", [8, 128, DCH, 512], F32, kind="ExternalInput")
    at = nc.dram_tensor("at", [128, DCH, 128], F32, kind="ExternalInput")
    bt = nc.dram_tensor("bt", [128, D], F32, kind="ExternalInput")
    bias_in = nc.dram_tensor("bias_in", [1, D], F32, kind="ExternalInput")
    lw_in = nc.dram_tensor("lw_in", [128, 1], F32, kind="ExternalInput")
    cos_in = nc.dram_tensor("cos_in", [128, TPC], F32, kind="ExternalInput")
    sin_in = nc.dram_tensor("sin_in", [128, TPC], F32, kind="ExternalInput")
    mask_in = nc.dram_tensor("mask_in", [128, NSLOT, 128], F32, kind="ExternalInput")

    out_t = nc.dram_tensor("out", [NCH, 128, 512], F32, kind="ExternalOutput")

    k_in = nc.dram_tensor("k_in", [NKVH * 128, TPC], BF16)
    v_in = nc.dram_tensor("v_in", [NKVH * TPC, 128], BF16)
    k_all = nc.dram_tensor("k_all", [N_CORES * NKVH * 128, TPC], BF16)
    v_all = nc.dram_tensor("v_all", [N_CORES * NKVH * TPC, 128], BF16)
    q_dram = nc.dram_tensor("q_dram", [D, TPC], BF16)
    g_dram = nc.dram_tensor("g_dram", [NCH, 128, 512], BF16)

    tap_t = {}
    tap_shapes = {
        "xn": [D, TPC], "kall": [N_CORES * NKVH * 128, TPC],
        "vall": [N_CORES * NKVH * TPC, 128],
        "q": [D, TPC], "ao": [D, TPC], "ffn3": [128, 16384], "g": [NCH, 128, 512],
        "wt01": [128, 32768],
    }
    for name in taps:
        tap_t[name] = nc.dram_tensor("tap_" + name, tap_shapes[name], F32,
                                     kind="ExternalOutput")

    es = ExitStack()
    with es:
        T1 = es.enter_context(nc.sbuf_tensor("T1", [128, 32768], BF16))
        T2 = es.enter_context(nc.sbuf_tensor("T2", [128, 32768], BF16))
        T3 = es.enter_context(nc.sbuf_tensor("T3", [128, 16384], BF16))
        T4 = es.enter_context(nc.sbuf_tensor("T4", [128, 4096], F32))
        cos_sb = es.enter_context(nc.sbuf_tensor("cos_sb", [128, TPC], BF16))
        sin_sb = es.enter_context(nc.sbuf_tensor("sin_sb", [128, TPC], BF16))
        mask_sb = es.enter_context(nc.sbuf_tensor("mask_sb", [128, NSLOT * 128], BF16))
        bias_sb = es.enter_context(nc.sbuf_tensor("bias_sb", [1, D], BF16))
        lw_sb = es.enter_context(nc.sbuf_tensor("lw_sb", [128, 1], F32))
        ones_col = es.enter_context(nc.sbuf_tensor("ones_col", [1, 128], BF16))
        ones_colf = es.enter_context(nc.sbuf_tensor("ones_colf", [1, 128], F32))
        ones_row = es.enter_context(nc.sbuf_tensor("ones_row", [128, 1], BF16))
        ident = es.enter_context(nc.sbuf_tensor("ident", [128, 128], BF16))
        rn_sb = es.enter_context(nc.sbuf_tensor("rn_sb", [128, TPC], F32))
        p_sb = es.enter_context(nc.sbuf_tensor("p_sb", [128, 2 * TPC], BF16))
        q_sb = es.enter_context(nc.sbuf_tensor("q_sb", [128, 2 * TPC], BF16))
        xtmp_sb = es.enter_context(nc.sbuf_tensor("xtmp_sb", [128, TPC], F32))

        sW = Sem(nc, es, "sW")
        sIO = Sem(nc, es, "sIO")
        sIO2 = Sem(nc, es, "sIO2")
        sWa = Sem(nc, es, "sWa")
        sWb = Sem(nc, es, "sWb")
        sVa = Sem(nc, es, "sVa")
        sVb = Sem(nc, es, "sVb")
        sKVa = Sem(nc, es, "sKVa")
        sKVb = Sem(nc, es, "sKVb")
        sQa = Sem(nc, es, "sQa")
        sQb = Sem(nc, es, "sQb")
        sGXa = Sem(nc, es, "sGXa")
        sGXb = Sem(nc, es, "sGXb")
        sOut = Sem(nc, es, "sOut")
        sMM = Sem(nc, es, "sMM")
        sEv = Sem(nc, es, "sEv")
        sVe = Sem(nc, es, "sVe")
        sPl = Sem(nc, es, "sPl")
        sCC = Sem(nc, es, "sCC")
        sCC2 = Sem(nc, es, "sCC2")
        sFSa = Sem(nc, es, "sFSa")
        sFSb = Sem(nc, es, "sFSb")
        sWT = Sem(nc, es, "sWT")
        sG = Sem(nc, es, "sG")

        btf = T4[:, 0:4096]
        bsc_sb = T3[:, 4096:8192]
        at_sb = T3[:, 8192:12288]
        h_sb = T3[:, 12288:13312]
        l_sb = T4[0:1, 0:TPC]
        lnl_sb = xtmp_sb

        def xch(j):
            return T1[:, j * 1024:(j + 1) * 1024]

        def xnch(j):
            return T2[:, j * 1024:(j + 1) * 1024]

        def aoch(h):
            return T1[:, h * 1024:(h + 1) * 1024]

        def pt(par):
            return p_sb[:, par * 1024:(par + 1) * 1024]

        def qt(par):
            return q_sb[:, par * 1024:(par + 1) * 1024]

        WBK = [0, 4096]
        WBV = [8192, 12288]

        def wkbuf(gi):
            return T3[:, WBK[gi % 2]:WBK[gi % 2] + 4096].rearrange("p (j n) -> p j n", n=128)

        def wvbuf(gi):
            return T3[:, WBV[gi % 2]:WBV[gi % 2] + 4096].rearrange("p (j n) -> p j n", n=128)

        WTB = [0, 16384]

        def wtbuf(b):
            return T1[:, WTB[b % 2]:WTB[b % 2] + 16384].rearrange("p (j n) -> p j n", n=512)

        def wqbuf(h):
            off = (h % 2) * 4096
            return T3[:, off:off + 4096].rearrange("p (j n) -> p j n", n=128)

        def wobuf(b):
            if b == 0:
                return T3[:, 0:16384].rearrange("p (j n) -> p j n", n=512)
            return T2[:, WTB[b % 2]:WTB[b % 2] + 16384].rearrange("p (j n) -> p j n", n=512)

        wk_ready = []
        wv_ready = []
        wt_ready = []
        wq_ready = []
        wo_ready = []
        at_ready = []
        castev = {}

        xcast = {f: ((sWT, sWT.n + f // 2 + 1) if f % 2 == 0
                     else (sVe, sVe.n + (f - 1) // 2 + 1)) for f in range(16)}
        xn_ready = [None] * DCH
        m_v = T4[0:1, 0:TPC]
        sq_v = T4[0:1, TPC:2 * TPC]
        rstd_v = T4[0:1, 2 * TPC:3 * TPC]
        negmr_v = T4[0:1, 3 * TPC:4 * TPC]
        msq_v = T4[0:1, 3 * TPC:4 * TPC]
        es12 = ExitStack()
        ps_k = es12.enter_context(nc.psum_tensor("ps_k", [128, TPC], F32))
        ps_b1 = ps_k
        with (
            nc.psum_tensor("ps_sx", [1, TPC], F32) as ps_sx,
            nc.psum_tensor("ps_sq", [1, TPC], F32) as ps_sq,
            nc.psum_tensor("ps_b2", [128, TPC], F32) as ps_b2,
            nc.Block("ln") as blk,
        ):
            mm0, ev0 = sMM.n, sEv.n
            vbase = sVe.n
            ve0 = vbase + 16
            pl0 = sPl.n
            frac_dma = []
            for j in range(DCH):
                xn_ready[j] = (sVe, ve0 + 4 + j + 1)
            sq_ready = [(sEv, ev0 + j + 1) if j < 24 else (sVe, vbase + 8 + (j - 24) + 1)
                        for j in range(DCH)]

            @blk.gpsimd
            def _(g):
                sW.inc(g.dma_start(out=cos_sb[:, :], in_=cos_in.ap()), 16)
                sW.inc(g.dma_start(out=sin_sb[:, :], in_=sin_in.ap()), 16)
                sW.inc(g.dma_start(out=mask_sb.rearrange("p (d q) -> p d q", q=128),
                                   in_=mask_in.ap()), 16)
                sW.inc(g.dma_start(out=bias_sb[:, :], in_=bias_in.ap()), 16)
                sW.inc(g.dma_start(out=lw_sb[:, :], in_=lw_in.ap()), 16)
                wk_ready.append((sWa, sWa.inc(g.dma_start(out=wkbuf(0), in_=wk.ap()[0]), 16)))
                wv_ready.append((sVa, sVa.inc(g.dma_start(out=wvbuf(0), in_=wv.ap()[0]), 16)))
                wk_ready.append((sWb, sWb.inc(g.dma_start(out=wkbuf(1), in_=wk.ap()[1]), 16)))
                wv_ready.append((sVb, sVb.inc(g.dma_start(out=wvbuf(1), in_=wv.ap()[1]), 16)))
                g.memset(ones_col[:, :], 1.0)
                g.memset(ones_colf[:, :], 1.0)
                g.memset(ones_row[:, :], 1.0)
                g.memset(ident[:, :], 0.0)
                sPl.inc(g.affine_select(out=ident[:, :], in_=ident[:, :],
                                        compare_op=ALU.not_equal, fill=1.0, base=0,
                                        pattern=[[-1, 128]], channel_multiplier=1))
                g.wait_ge(sW.h, sW.n)

            @blk.sync
            def _(sy):
                for f in range(16):
                    sp = sFSa if f % 2 == 0 else sFSb
                    if f >= 2:
                        sy.wait_ge(xcast[f - 2][0].h, xcast[f - 2][1])
                    srcx = x_fm.ap().rearrange("(c p) t -> p c t", p=128)[:, 2 * f:2 * f + 2, :]
                    dst = T4[:, (f % 2) * 2048:(f % 2) * 2048 + 2048].rearrange(
                        "p (c t) -> p c t", t=1024)
                    frac_dma.append((sp, sp.inc(sy.dma_start(out=dst, in_=srcx), 16)))

            @blk.scalar
            def _(s):
                for f in range(16):
                    if f % 2 == 0:
                        s.wait_ge(frac_dma[f][0].h, frac_dma[f][1])
                        assert xcast[f][1] == sWT.inc(s.copy(
                            T1[:, f * 2048:(f + 1) * 2048], T4[:, 0:2048]))
                    elif f < 12:
                        s.wait_ge(xcast[f][0].h, xcast[f][1])
                    if f < 12:
                        for j in (2 * f, 2 * f + 1):
                            if j >= 2:
                                s.wait_ge(sMM.h, mm0 + j - 1)
                            sEv.inc(s.activation(qt(j % 2), xch(j), AF.Square))
                s.wait_ge(sVe.h, ve0 + 1)
                s.activation(rstd_v, sq_v, AF.Ln, bias=np.float32(LN_EPS))
                sEv.inc(s.activation(rstd_v, rstd_v, AF.Exp, scale=np.float32(-0.5)))

            @blk.tensor
            def _(t):
                t.wait_ge(sPl.h, pl0 + 1)
                for j in range(DCH):
                    if j % 2 == 0:
                        t.wait_ge(xcast[j // 2][0].h, xcast[j // 2][1])
                    t.matmul(ps_sx[:, 0:512], ones_row[:, :], xch(j)[:, 0:512],
                             start=(j == 0), stop=(j == DCH - 1))
                    t.matmul(ps_sx[:, 512:1024], ones_row[:, :], xch(j)[:, 512:1024],
                             start=(j == 0), stop=(j == DCH - 1))
                    t.wait_ge(sq_ready[j][0].h, sq_ready[j][1])
                    sq_src = qt(j % 2)
                    t.matmul(ps_sq[:, 0:512], ones_row[:, :], sq_src[:, 0:512],
                             start=(j == 0), stop=(j == DCH - 1))
                    sMM.inc(t.matmul(ps_sq[:, 512:1024], ones_row[:, :], sq_src[:, 512:1024],
                                     start=(j == 0), stop=(j == DCH - 1)))
                t.wait_ge(sVe.h, ve0 + 2)
                t.matmul(ps_b1[:, 0:512], ones_colf[:, :], rstd_v[:, 0:512], start=True, stop=True)
                t.matmul(ps_b1[:, 512:1024], ones_colf[:, :], rstd_v[:, 512:1024], start=True, stop=True)
                t.matmul(ps_b2[:, 0:512], ones_colf[:, :], negmr_v[:, 0:512], start=True, stop=True)
                sMM.inc(t.matmul(ps_b2[:, 512:1024], ones_colf[:, :], negmr_v[:, 512:1024],
                                 start=True, stop=True))
                t.wait_ge(wk_ready[0][0].h, wk_ready[0][1])
                for j in range(DCH):
                    t.wait_ge(xn_ready[j][0].h, xn_ready[j][1])
                    for half in range(2):
                        ins = t.matmul(ps_k[:, half * 512:half * 512 + 512],
                                       wkbuf(0)[:, j, :],
                                       xnch(j)[:, half * 512:half * 512 + 512],
                                       start=(j == 0), stop=(j == DCH - 1))
                sMM.inc(ins)

            @blk.vector
            def _(v):
                for f in range(1, 16, 2):
                    v.wait_ge(frac_dma[f][0].h, frac_dma[f][1])
                    assert xcast[f][1] == sVe.inc(v.tensor_copy(
                        T1[:, f * 2048:(f + 1) * 2048], T4[:, 2048:4096]))
                for j in range(24, DCH):
                    if j % 2 == 0:
                        v.wait_ge(xcast[j // 2][0].h, xcast[j // 2][1])
                    v.wait_ge(sMM.h, mm0 + j - 1)
                    assert sq_ready[j][1] == sVe.inc(
                        v.tensor_tensor(qt(j % 2), xch(j), xch(j), ALU.mult))
                v.wait_ge(sMM.h, mm0 + DCH)
                v.tensor_scalar_mul(m_v, ps_sx[:, :], 1.0 / D)
                v.tensor_scalar_mul(sq_v, ps_sq[:, :], 1.0 / D)
                v.tensor_tensor(msq_v, m_v, m_v, ALU.mult)
                sVe.inc(v.tensor_tensor(sq_v, sq_v, msq_v, ALU.subtract))
                v.wait_ge(sEv.h, ev0 + 25)
                sVe.inc(v.scalar_tensor_tensor(negmr_v, m_v, -1.0, rstd_v, ALU.mult, ALU.mult))
                v.wait_ge(sMM.h, mm0 + DCH + 1)
                sVe.inc(v.tensor_copy(p_sb[:, 0:TPC], ps_b1[:, :]))
                sVe.inc(v.tensor_copy(p_sb[:, TPC:2 * TPC], ps_b2[:, :]))
                for j in range(DCH):
                    v.tensor_tensor(q_sb[:, 0:1024], xch(j), p_sb[:, 0:TPC], ALU.mult)
                    sVe.inc(v.tensor_tensor(xnch(j), q_sb[:, 0:1024], p_sb[:, TPC:2 * TPC], ALU.add))


        if "xn" in tap_t:
            with nc.Block("tap_xn") as blk:
                @blk.gpsimd
                def _(g):
                    dst = tap_t["xn"].ap().rearrange("(c p) t -> p c t", p=128)
                    srcx = T2.rearrange("p (c t) -> p c t", t=1024)
                    sW.inc(g.dma_start(out=dst, in_=srcx), 16)
                    g.wait_ge(sW.h, sW.n)

        with (
            nc.psum_tensor("ps_v", [128, TPC], F32) as ps_v,
            nc.psum_tensor("ps_tr0", [128, 128], BF16) as ps_tr0,
            nc.psum_tensor("ps_tr1", [128, 128], BF16) as ps_tr1,
            nc.Block("kv") as blk,
        ):
            mm0, ev0, ve0, w0, io0 = sMM.n, sEv.n, sVe.n, sW.n, sIO.n
            ps_tr = [ps_tr0, ps_tr1]
            kmm = [mm0 + gi for gi in range(NKVH)]
            vmm = [mm0 + 7 + 9 * gi + 1 for gi in range(NKVH)]
            vtr = [[mm0 + 7 + 9 * gi + 2 + c for c in range(8)] for gi in range(NKVH)]
            kev = [ev0 + gi + 1 for gi in range(NKVH)]
            vev = [ev0 + 8 + gi + 1 for gi in range(NKVH)]
            kve = [ve0 + gi + 1 for gi in range(NKVH)]
            vtc = [[ve0 + 8 + 8 * gi + c + 1 for c in range(8)] for gi in range(NKVH)]
            kdma = [io0 + 16 * (gi + 1) for gi in range(NKVH)]
            vdma = [io0 + 128 + 16 * (gi + 1) for gi in range(NKVH)]
            wt0 = sWT.n
            castev.update({k: (sWT, wt0 + k + 1) for k in range(16)})
            veP1, plP1 = sVe.n, sPl.n
            fsa0, fsb0 = sFSa.n, sFSb.n
            fracv = {k: ((sFSa, fsa0 + 16 * (k // 2 + 1)) if k % 2 == 0
                         else (sFSb, fsb0 + 16 * (k // 2 + 1))) for k in range(16)}

            @blk.gpsimd
            def _(g):
                for gi in range(2, NKVH):
                    g.wait_ge(sMM.h, kmm[gi - 2])
                    sp = sWa if gi % 2 == 0 else sWb
                    wk_ready.append((sp, sp.inc(g.dma_start(out=wkbuf(gi), in_=wk.ap()[gi]), 16)))
                for gi in range(2, NKVH):
                    g.wait_ge(sMM.h, vmm[gi - 2])
                    sp = sVa if gi % 2 == 0 else sVb
                    wv_ready.append((sp, sp.inc(g.dma_start(out=wvbuf(gi), in_=wv.ap()[gi]), 16)))
                g.wait_ge(sMM.h, vmm[6])
                at_ready.append((sW, sW.inc(g.dma_start(
                    out=at_sb.rearrange("p (j n) -> p j n", n=128), in_=at.ap()), 16)))
                g.wait_ge(sIO.h, io0 + 256)
                sCC.inc(g.collective_compute(
                    "AllGather", ALU.bypass, replica_groups=[list(range(N_CORES))],
                    ins=[k_in.ap().opt()], outs=[k_all.ap().opt()]))
                sCC2.inc(g.collective_compute(
                    "AllGather", ALU.bypass, replica_groups=[list(range(N_CORES))],
                    ins=[v_in.ap().opt()], outs=[v_all.ap().opt()]))

            @blk.tensor
            def _(t):
                for gi in range(1, NKVH):
                    t.wait_ge(wk_ready[gi][0].h, wk_ready[gi][1])
                    t.wait_ge(sEv.h, kev[gi - 1])
                    for j in range(DCH):
                        for half in range(2):
                            ins = t.matmul(ps_k[:, half * 512:half * 512 + 512],
                                           wkbuf(gi)[:, j, :],
                                           xnch(j)[:, half * 512:half * 512 + 512],
                                           start=(j == 0), stop=(j == DCH - 1))
                    sMM.inc(ins)
                for gi in range(NKVH):
                    t.wait_ge(wv_ready[gi][0].h, wv_ready[gi][1])
                    if gi >= 1:
                        t.wait_ge(sEv.h, vev[gi - 1])
                    for j in range(DCH):
                        for half in range(2):
                            ins = t.matmul(ps_v[:, half * 512:half * 512 + 512],
                                           wvbuf(gi)[:, j, :],
                                           xnch(j)[:, half * 512:half * 512 + 512],
                                           start=(j == 0), stop=(j == DCH - 1))
                    sMM.inc(ins)
                    t.wait_ge(sEv.h, vev[gi])
                    for c in range(8):
                        if c >= 2:
                            t.wait_ge(sVe.h, vtc[gi][c - 2])
                        elif gi >= 1 and c < 2:
                            t.wait_ge(sVe.h, vtc[gi - 1][c + 6])
                        sMM.inc(t.transpose(ps_tr[c % 2][:, :],
                                            pt(gi % 2)[:, c * 128:(c + 1) * 128],
                                            ident[:, :]))

            @blk.scalar
            def _(s):
                for gi in range(NKVH):
                    s.wait_ge(sMM.h, kmm[gi])
                    if gi >= 2:
                        s.wait_ge(sIO.h, kdma[gi - 2])
                    sEv.inc(s.copy(pt(gi % 2), ps_k[:, :]))
                    for k in (2 * gi, 2 * gi + 1):
                        b, f = divmod(k, 8)
                        s.wait_ge(fracv[k][0].h, fracv[k][1])
                        stg = T4[:, (k % 2) * 2048:(k % 2) * 2048 + 2048].rearrange(
                            "p (j n) -> p j n", n=512)
                        assert castev[k][1] == sWT.inc(s.copy(
                            wtbuf(b)[:, 4 * f:4 * f + 4, :], stg))
                for gi in range(NKVH):
                    s.wait_ge(sMM.h, vmm[gi])
                    if gi == 0:
                        s.wait_ge(sIO.h, kdma[6])
                    elif gi == 1:
                        s.wait_ge(sIO.h, kdma[7])
                    else:
                        s.wait_ge(sIO.h, vdma[gi - 2])
                    sEv.inc(s.copy(pt(gi % 2), ps_v[:, :]))

            @blk.vector
            def _(v):
                for gi in range(NKVH):
                    v.wait_ge(sEv.h, kev[gi])
                    kraw = pt(gi % 2)
                    krot = qt(gi % 2)
                    v.stream_shuffle(krot, kraw, SWAP_MASK)
                    v.tensor_tensor(xtmp_sb[:, :], kraw, cos_sb[:, :], ALU.mult)
                    v.tensor_tensor(rn_sb[:, :], krot, sin_sb[:, :], ALU.mult)
                    sVe.inc(v.tensor_tensor(kraw, xtmp_sb[:, :], rn_sb[:, :], ALU.add))
                for gi in range(NKVH):
                    if gi == 0:
                        v.wait_ge(sIO.h, kdma[6])
                    elif gi == 1:
                        v.wait_ge(sIO.h, kdma[7])
                    else:
                        v.wait_ge(sIO.h, vdma[gi - 2])
                    for c in range(8):
                        v.wait_ge(sMM.h, vtr[gi][c])
                        sVe.inc(v.tensor_copy(qt(gi % 2)[:, c * 128:(c + 1) * 128],
                                              ps_tr[c % 2][:, :]))

            @blk.sync
            def _(sy):
                def frac_issue(k):
                    b, f = divmod(k, 8)
                    sp = sFSa if k % 2 == 0 else sFSb
                    if k >= 2:
                        sy.wait_ge(castev[k - 2][0].h, castev[k - 2][1])
                    stg = T4[:, (k % 2) * 2048:(k % 2) * 2048 + 2048].rearrange(
                        "p (j n) -> p j n", n=512)
                    assert fracv[k][1] == sp.inc(sy.dma_start(
                        out=stg, in_=wt.ap()[b][:, 4 * f:4 * f + 4, :]), 16)

                sy.wait_ge(sVe.h, veP1)
                sy.wait_ge(sPl.h, plP1)
                frac_issue(0)
                frac_issue(1)
                for gi in range(NKVH):
                    sy.wait_ge(sVe.h, kve[gi])
                    sy.wait_ge(sIO.h, sIO.n)
                    sIO.inc(sy.dma_start(out=k_in.ap()[gi * 128:(gi + 1) * 128, :],
                                         in_=pt(gi % 2)), 16)
                    if gi < 7:
                        frac_issue(2 * gi + 2)
                        frac_issue(2 * gi + 3)
                for gi in range(NKVH):
                    sy.wait_ge(sVe.h, vtc[gi][7])
                    sy.wait_ge(sIO.h, sIO.n)
                    dst = v_in.ap()[gi * 1024:(gi + 1) * 1024, :].rearrange(
                        "(c p) h -> p c h", p=128)
                    src = qt(gi % 2).rearrange("p (c h) -> p c h", h=128)
                    sIO.inc(sy.dma_start(out=dst, in_=src), 16)
                sy.wait_ge(sIO.h, sIO.n)

        if "wt01" in tap_t:
            with nc.Block("tap_wt01") as blk:
                @blk.gpsimd
                def _(g):
                    sW.inc(g.dma_start(out=tap_t["wt01"].ap(), in_=T1[:, :]), 16)
                    g.wait_ge(sW.h, sW.n)

        es12.close()

        with (
            nc.psum_tensor("ps_h", [128, TPC], F32) as ps_h,
            nc.psum_tensor("ps_f0", [128, 512], F32) as ps_f0,
            nc.psum_tensor("ps_f1", [128, 512], F32) as ps_f1,
            nc.Block("ffn") as blk,
        ):
            mm0, ev0, ve0, w0, g0 = sMM.n, sEv.n, sVe.n, sW.n, sG.n
            ps_f = [ps_f0, ps_f1]
            fmm = [mm0 + 2 + it + 1 for it in range(64)]
            fsil = [ev0 + 1 + it + 1 for it in range(64)]

            bt_ev = []
            frac = []
            castev.update({k: (sVe, sVe.n + 2 + (k - 16)) for k in range(16, 64)})
            for b in range(8):
                wt_ready.append(castev[b * 8 + 7])

            @blk.gpsimd
            def _(g):
                def g_store(it):
                    g.wait_ge(sEv.h, fsil[it])
                    g.wait_ge(sG.h, sG.n)
                    sG.inc(g.dma_start(
                        out=g_dram.ap()[it],
                        in_=p_sb[:, (it % 4) * 512:(it % 4) * 512 + 512]), 16)

                for it in range(8):
                    g_store(it)
                wq_ready.append((sVa, sVa.inc(g.dma_start(out=wqbuf(0), in_=wq.ap()[0]), 16)))
                for it in range(8, 64):
                    g_store(it)
                g.wait_ge(sG.h, sG.n)
                g.wait_ge(sW.h, sW.n)

            @blk.sync
            def _(sy):
                def frac_issue(k):
                    b, f = divmod(k, 8)
                    sp = sFSa if k % 2 == 0 else sFSb
                    sy.wait_ge(castev[k - 2][0].h, castev[k - 2][1])
                    stg = T4[:, (k % 2) * 2048:(k % 2) * 2048 + 2048].rearrange(
                        "p (j n) -> p j n", n=512)
                    frac.append((sp, sp.inc(sy.dma_start(
                        out=stg, in_=wt.ap()[b][:, 4 * f:4 * f + 4, :]), 16)))

                bt_ev.append((sFSa, sFSa.inc(sy.dma_start(out=btf, in_=bt.ap()), 16)))
                sy.wait_ge(sVe.h, ve0 + 1)
                for k in range(16, 64):
                    frac_issue(k)

            @blk.tensor
            def _(t):
                t.wait_ge(at_ready[0][0].h, at_ready[0][1])
                for j in range(DCH):
                    for half in range(2):
                        ins = t.matmul(ps_h[:, half * 512:half * 512 + 512],
                                       at_sb[:, j * 128:(j + 1) * 128],
                                       xnch(j)[:, half * 512:half * 512 + 512],
                                       start=(j == 0), stop=(j == DCH - 1))
                        if j == DCH - 1:
                            sMM.inc(ins)
                t.wait_ge(sEv.h, ev0 + 1)
                t.wait_ge(sVe.h, ve0 + 1)
                for b in range(8):
                    t.wait_ge(wt_ready[b][0].h, wt_ready[b][1])
                    for ti in range(8):
                        it = b * 8 + ti
                        if it >= 2:
                            t.wait_ge(sEv.h, fsil[it - 2])
                        for j in range(DCH):
                            t.matmul(ps_f[it % 2][:, :],
                                     xnch(j)[:, ti * 128:(ti + 1) * 128],
                                     wtbuf(b)[:, j, :],
                                     start=(j == 0), stop=False)
                        t.matmul(ps_f[it % 2][:, :],
                                 h_sb[:, ti * 128:(ti + 1) * 128],
                                 bsc_sb[:, b * 512:(b + 1) * 512],
                                 start=False, stop=False)
                        sMM.inc(t.matmul(ps_f[it % 2][:, :],
                                         ones_col[:, :],
                                         bias_sb[:, b * 512:(b + 1) * 512],
                                         start=False, stop=True))

            @blk.vector
            def _(v):
                v.wait_ge(bt_ev[0][0].h, bt_ev[0][1])
                sVe.inc(v.tensor_scalar(bsc_sb[:, :], btf, lw_sb[:, :],
                                        2.0, ALU.mult, ALU.mult))
                for k in range(16, 64):
                    b, f = divmod(k, 8)
                    v.wait_ge(frac[k - 16][0].h, frac[k - 16][1])
                    if f == 0 and b >= 2:
                        v.wait_ge(sMM.h, fmm[(b - 2) * 8 + 7])
                    stg = T4[:, (k % 2) * 2048:(k % 2) * 2048 + 2048].rearrange(
                        "p (j n) -> p j n", n=512)
                    assert castev[k][1] == sVe.inc(v.tensor_copy(
                        wtbuf(b)[:, 4 * f:4 * f + 4, :], stg))

            @blk.scalar
            def _(s):
                s.wait_ge(sMM.h, mm0 + 2)
                sEv.inc(s.copy(h_sb[:, :], ps_h[:, :]))
                for b in range(8):
                    for ti in range(8):
                        it = b * 8 + ti
                        s.wait_ge(sMM.h, fmm[it])
                        slot = p_sb[:, (it % 4) * 512:(it % 4) * 512 + 512]
                        if it >= 4:
                            s.wait_ge(sG.h, g0 + 16 * (it - 3))
                        sEv.inc(s.activation(slot, ps_f[it % 2][:, :], AF.Silu))

            if "g" in tap_t:
                @blk.sync
                def _(sy):
                    sy.wait_ge(sEv.h, sEv.n)

        if "ffn3" in tap_t:
            with nc.Block("tap_ffn3") as blk:
                @blk.gpsimd
                def _(g):
                    sW.inc(g.dma_start(out=tap_t["ffn3"].ap(), in_=T3[:, :]), 16)
                    g.wait_ge(sW.h, sW.n)

        if "g" in tap_t:
            with nc.Block("tap_g") as blk:
                @blk.gpsimd
                def _(g):
                    sW.inc(g.dma_start(out=tap_t["g"].ap(), in_=g_dram.ap()), 16)
                    g.wait_ge(sW.h, sW.n)

        with (
            nc.psum_tensor("ps_q", [128, TPC], F32) as ps_q,
            nc.Block("qproj") as blk,
        ):
            mm0, ev0, ve0, w0, io0 = sMM.n, sEv.n, sVe.n, sW.n, sIO.n
            qmm = [mm0 + h + 1 for h in range(NQH)]
            qev = [ev0 + h + 1 for h in range(NQH)]
            qve = [ve0 + h + 1 for h in range(NQH)]
            qdma = [io0 + 16 * (h + 1) for h in range(NQH)]

            @blk.gpsimd
            def _(g):
                for h in (1,):
                    wq_ready.append((sVb, sVb.inc(g.dma_start(out=wqbuf(h), in_=wq.ap()[h]), 16)))
                for h in range(2, NQH):
                    g.wait_ge(sMM.h, qmm[h - 2])
                    sp = sVa if h % 2 == 0 else sVb
                    wq_ready.append((sp, sp.inc(g.dma_start(out=wqbuf(h), in_=wq.ap()[h]), 16)))

            @blk.tensor
            def _(t):
                for h in range(NQH):
                    t.wait_ge(wq_ready[h][0].h, wq_ready[h][1])
                    if h >= 1:
                        t.wait_ge(sEv.h, qev[h - 1])
                    for j in range(DCH):
                        for half in range(2):
                            ins = t.matmul(ps_q[:, half * 512:half * 512 + 512],
                                           wqbuf(h)[:, j, :],
                                           xnch(j)[:, half * 512:half * 512 + 512],
                                           start=(j == 0), stop=(j == DCH - 1))
                    sMM.inc(ins)

            @blk.scalar
            def _(s):
                for h in range(NQH):
                    s.wait_ge(sMM.h, qmm[h])
                    if h >= 2:
                        s.wait_ge(sIO.h, qdma[h - 2])
                    sEv.inc(s.copy(pt(h % 2), ps_q[:, :]))

            @blk.vector
            def _(v):
                for h in range(NQH):
                    v.wait_ge(sEv.h, qev[h])
                    qraw = pt(h % 2)
                    qrot = qt(h % 2)
                    v.stream_shuffle(qrot, qraw, SWAP_MASK)
                    v.tensor_tensor(xtmp_sb[:, :], qraw, cos_sb[:, :], ALU.mult)
                    v.tensor_tensor(rn_sb[:, :], qrot, sin_sb[:, :], ALU.mult)
                    sVe.inc(v.tensor_tensor(qraw, xtmp_sb[:, :], rn_sb[:, :], ALU.add))

            @blk.sync
            def _(sy):
                sy.wait_ge(sCC.h, 1)
                sy.wait_ge(sCC2.h, 1)
                for r in range(N_CORES):
                    srck = k_all.ap()[r * 1024:r * 1024 + 128, :] \
                        .rearrange("q (m t) -> q m t", t=128)
                    dstk = T1[:, 16384:24576].rearrange("p (m rr t) -> p m rr t", rr=8, t=128)[:, :, r, :]
                    sPl.inc(sy.dma_start(out=dstk, in_=srck), 16)
                    srcv = v_all.ap()[r * 8192:r * 8192 + 1024, :] \
                        .rearrange("(m p) h -> p m h", p=128)
                    dstv = T1[:, 24576:32768].rearrange("p (m rr t) -> p m rr t", rr=8, t=128)[:, :, r, :]
                    sPl.inc(sy.dma_start(out=dstv, in_=srcv), 16)
                for h in range(NQH):
                    sy.wait_ge(sVe.h, qve[h])
                    sy.wait_ge(sIO.h, sIO.n)
                    sIO.inc(sy.dma_start(out=q_dram.ap()[h * 128:(h + 1) * 128, :],
                                         in_=pt(h % 2)), 16)
                sy.wait_ge(sIO.h, sIO.n)

        for name, src_d in (("q", q_dram), ("kall", k_all), ("vall", v_all)):
            if name in tap_t:
                with nc.Block("tap_" + name) as blk:
                    @blk.gpsimd
                    def _(g, src_d=src_d, name=name):
                        if name in ("kall", "vall"):
                            g.wait_ge(sCC.h, 1)
                            g.wait_ge(sCC2.h, 1)
                        sW.inc(g.dma_start(out=tap_t[name].ap(), in_=src_d.ap()), 16)
                        g.wait_ge(sW.h, sW.n)

        def KG(g):
            if g == 0:
                return T1[:, 16384:24576]
            b = (g - 1) % 2
            return T2[:, b * 16384:b * 16384 + 8192]

        def VG(g):
            if g == 0:
                return T1[:, 24576:32768]
            b = (g - 1) % 2
            return T2[:, b * 16384 + 8192:(b + 1) * 16384]

        with (
            nc.psum_tensor("ps_at_a0", [128, 512], F32) as ps_a0,
            nc.psum_tensor("ps_at_a1", [128, 512], F32) as ps_a1,
            nc.psum_tensor("ps_at_b0", [128, 512], F32) as ps_b0,
            nc.psum_tensor("ps_at_b1", [128, 512], F32) as ps_b1,
            nc.psum_tensor("ps_avA", [128, 512], F32) as ps_avA,
            nc.psum_tensor("ps_avB", [128, 512], F32) as ps_avB,
            nc.psum_tensor("ps_lA", [1, 512], F32) as ps_lA,
            nc.psum_tensor("ps_lB", [1, 512], F32) as ps_lB,
            nc.Block("attn") as blk,
        ):
            NH = NQH
            SCB = {"A0": ps_a0, "A1": ps_a1, "B0": ps_b0, "B1": ps_b1}

            def ptile(r):
                return p_sb[:, r * 512:(r + 1) * 512]

            lrow = l_sb[0:1, 0:512]
            lnrow = l_sb[0:1, 512:1024]

            tplan, splan, vplan = [], [], []
            for H in range(NH):
                for step in range(67):
                    if step == 38 and H >= 1:
                        tplan.append(("bcast", H - 1, 1))
                    if step == 50:
                        tplan.append(("bcast", H, 0))
                    if step <= 63:
                        j = step
                        if j < 32:
                            tplan.append(("scores", H, j, 0, "A0" if j % 2 == 0 else "A1"))
                            tplan.append(("scores", H, j, 1, "B0" if j % 2 == 0 else "B1"))
                        else:
                            tplan.append(("scores", H, j, 1, ["B0", "B1", "A0"][(j - 32) % 3]))
                    jj = step - 2
                    if 0 <= jj < 32:
                        tplan.append(("avsum", H, jj, 1, 2 + jj % 2))
                        tplan.append(("avsum", H, jj, 0, jj % 2))
                    jj = step - 3
                    if 32 <= jj <= 63:
                        tplan.append(("avsum", H, jj, 1, (jj - 32) % 4))
            tplan.append(("bcast", NH - 1, 1))

            for H in range(NH):
                if H >= 1:
                    splan.append(("evac", H - 1, 1))
                for j in range(64):
                    if j == 2 and H >= 1:
                        splan.append(("ln", H - 1, 1))
                    if j == 3 and H >= 1:
                        splan.append(("rn", H - 1, 1))
                    if j == 36:
                        splan.append(("evac", H, 0))
                    if j == 44:
                        splan.append(("ln", H, 0))
                    if j == 46:
                        splan.append(("rn", H, 0))
                    if j < 32:
                        splan.append(("exp", H, j, 0))
                    splan.append(("exp", H, j, 1))
            splan += [("evac", NH - 1, 1), ("ln", NH - 1, 1), ("rn", NH - 1, 1)]

            for H in range(NH):
                if H >= 1:
                    vplan.append(("lcopy", H - 1, 1))
                for j in range(64):
                    if j == 38:
                        vplan.append(("lcopy", H, 0))
                    if j == 40 and H >= 1:
                        vplan.append(("mult", H - 1, 1))
                    if j == 52:
                        vplan.append(("mult", H, 0))
                    vplan.append(("mask", H, j))
            vplan += [("lcopy", NH - 1, 1), ("mult", NH - 1, 1)]

            mm0, ev0, ve0 = sMM.n, sEv.n, sVe.n
            SCc = {}; ASc = {}; BCc = {}; AS0last = {}; AS1last = {}; SC1last = {}
            c = mm0
            for op in tplan:
                c += 1
                if op[0] == "scores":
                    _, H, j, h, bk = op
                    SCc[(H, j, h)] = c
                    if h == 1 and j == 63:
                        SC1last[H] = c
                elif op[0] == "avsum":
                    _, H, j, h, pt_ = op
                    ASc[(H, j, h)] = c
                    if h == 0 and j == 31:
                        AS0last[H] = c
                    if h == 1 and j == 63:
                        AS1last[H] = c
                else:
                    _, w, h = op
                    BCc[(w, h)] = c
            mm_end = c
            EXc = {}; EVAC = {}; RNc = {}
            c = ev0
            for op in splan:
                c += 1
                if op[0] == "exp":
                    EXc[(op[1], op[2], op[3])] = c
                elif op[0] == "evac":
                    EVAC[(op[1], op[2])] = c
                elif op[0] == "rn":
                    RNc[(op[1], op[2])] = c
            ev_end = c
            MKc = {}; LCc = {}; MUc = {}
            c = ve0
            for op in vplan:
                c += 1
                if op[0] == "mask":
                    MKc[(op[1], op[2])] = c
                elif op[0] == "lcopy":
                    LCc[(op[1], op[2])] = c
                else:
                    MUc[(op[1], op[2])] = c
            ve_end = c

            twaits = []
            bank_pend = {k: [] for k in ("A0", "A1", "B0", "B1")}
            pt_prev = {}
            exp_war = {}
            for op in tplan:
                w = []
                if op[0] == "scores":
                    _, H, j, h, bk = op
                    w += bank_pend[bk]
                    bank_pend[bk] = [("sEv", EXc[(H, j, h)])]
                    if (H, j) == (0, 0) and h == 0:
                        pass
                elif op[0] == "avsum":
                    _, H, j, h, pt_ = op
                    if h == 0 or j >= 32:
                        w.append(("sVe", MKc[(H, j)]))
                    else:
                        w.append(("sEv", EXc[(H, j, 1)]))
                    if j == 0 and H >= 1:
                        w.append(("sEv", EVAC[(H - 1, h)]))
                        w.append(("sVe", LCc[(H - 1, h)]))
                    exp_war[(H, j, h)] = pt_prev.get(pt_)
                    pt_prev[pt_] = ("sMM", ASc[(H, j, h)])
                else:
                    _, wd, h = op
                    w.append(("sEv", RNc[(wd, h)]))
                    w += bank_pend["A1"]
                    bank_pend["A1"] = [("sVe", MUc[(wd, h)])]
                twaits.append(w)
            swaits = []
            for op in splan:
                w = []
                if op[0] == "exp":
                    _, H, j, h = op
                    w.append(("sMM", SCc[(H, j, h)]))
                    if exp_war[(H, j, h)] is not None:
                        w.append(exp_war[(H, j, h)])
                elif op[0] == "evac":
                    _, H, h = op
                    w.append(("sMM", AS0last[H] if h == 0 else AS1last[H]))
                elif op[0] == "ln":
                    _, H, h = op
                    w.append(("sVe", LCc[(H, h)]))
                swaits.append(w)
            vwaits = []
            chain_prev_bc = {}
            chain = [(H, h) for H in range(NH) for h in (0, 1)]
            prev_chain = {chain[i]: chain[i - 1] for i in range(1, len(chain))}
            for op in vplan:
                w = []
                if op[0] == "mask":
                    _, H, j = op
                    hd = 0 if j < 32 else 1
                    w.append(("sEv", EXc[(H, j, hd)]))
                elif op[0] == "lcopy":
                    _, H, h = op
                    w.append(("sMM", AS0last[H] if h == 0 else AS1last[H]))
                    if (H, h) in prev_chain:
                        w.append(("sMM", BCc[prev_chain[(H, h)]]))
                else:
                    _, H, h = op
                    w.append(("sMM", BCc[(H, h)]))
                    w.append(("sEv", EVAC[(H, h)]))
                vwaits.append(w)

            kv_ready = {}
            q_ready = {}

            def do_waits(eng, w):
                for kind, val in w:
                    if kind == "sMM":
                        eng.wait_ge(sMM.h, val)
                    elif kind == "sEv":
                        eng.wait_ge(sEv.h, val)
                    elif kind == "sVe":
                        eng.wait_ge(sVe.h, val)

            @blk.sync
            def _(sy):
                def kv_load(g):
                    sp = sKVa if (g - 1) % 2 == 0 else sKVb
                    for r in range(N_CORES):
                        srck = k_all.ap()[r * 1024 + g * 128:r * 1024 + (g + 1) * 128, :] \
                            .rearrange("q (m t) -> q m t", t=128)
                        dstk = KG(g).rearrange("p (m rr t) -> p m rr t", rr=8, t=128)[:, :, r, :]
                        sp.inc(sy.dma_start(out=dstk, in_=srck), 16)
                        srcv = v_all.ap()[r * 8192 + g * 1024:r * 8192 + (g + 1) * 1024, :] \
                            .rearrange("(m p) h -> p m h", p=128)
                        dstv = VG(g).rearrange("p (m rr t) -> p m rr t", rr=8, t=128)[:, :, r, :]
                        sp.inc(sy.dma_start(out=dstv, in_=srcv), 16)
                    kv_ready[g] = (sp, sp.n)

                def q_load(H):
                    sp = sQa if H % 2 == 0 else sQb
                    sp.inc(sy.dma_start(out=qt(H % 2),
                                        in_=q_dram.ap()[H * 128:(H + 1) * 128, :]), 16)
                    q_ready[H] = (sp, sp.n)

                kv_ready[0] = (sPl, sPl.n)
                q_load(0)
                q_load(1)
                kv_load(1)
                kv_load(2)
                ev_list = []
                for g in range(3, NKVH):
                    ev_list.append((AS1last[4 * (g - 1) - 1], "kv", g))
                for H in range(2, NH):
                    ev_list.append((SC1last[H - 2], "q", H))
                ev_list.sort(key=lambda x: x[0])
                for wv_, kind, idx in ev_list:
                    sy.wait_ge(sMM.h, wv_)
                    if kind == "kv":
                        kv_load(idx)
                    else:
                        q_load(idx)

            @blk.tensor
            def _(t):
                for op, w in zip(tplan, twaits):
                    do_waits(t, w)
                    if op[0] == "scores":
                        _, H, j, h, bk = op
                        if j == 0 and h <= 1 and op is not None:
                            pass
                        c0 = (j // 8) * 128
                        lo = c0 if h == 0 else max(0, c0 - 512)
                        if j == 0 and h == 0:
                            if H % 4 == 0:
                                t.wait_ge(kv_ready[H // 4][0].h, kv_ready[H // 4][1])
                            t.wait_ge(q_ready[H][0].h, q_ready[H][1])
                        kg = KG(H // 4)
                        sMM.inc(t.matmul(SCB[bk][:, lo:512],
                                         kg[:, j * 128:(j + 1) * 128],
                                         qt(H % 2)[:, h * 512 + lo:h * 512 + 512],
                                         start=True, stop=True))
                    elif op[0] == "avsum":
                        _, H, j, h, pt_ = op
                        c0 = (j // 8) * 128
                        lo = c0 if h == 0 else max(0, c0 - 512)
                        av = ps_avA if h == 0 else ps_avB
                        lb = ps_lA if h == 0 else ps_lB
                        first = (j == 0)
                        lastf = (j == 31) if h == 0 else (j == 63)
                        vg = VG(H // 4)
                        p = ptile(pt_)
                        t.matmul(av[:, lo:512], vg[:, j * 128:(j + 1) * 128],
                                 p[:, lo:512], start=first, stop=lastf)
                        sMM.inc(t.matmul(lb[:, lo:512], ones_row[:, :],
                                         p[:, lo:512], start=first, stop=lastf))
                    else:
                        _, wd, h = op
                        sMM.inc(t.matmul(ps_a1[:, :], ones_colf[:, :], lrow,
                                         start=True, stop=True))

            @blk.scalar
            def _(s):
                for op, w in zip(splan, swaits):
                    do_waits(s, w)
                    if op[0] == "exp":
                        _, H, j, h = op
                        c0 = (j // 8) * 128
                        lo = c0 if h == 0 else max(0, c0 - 512)
                        if h == 0:
                            bk = "A0" if j % 2 == 0 else "A1"
                            pt_ = j % 2
                        elif j < 32:
                            bk = "B0" if j % 2 == 0 else "B1"
                            pt_ = 2 + j % 2
                        else:
                            bk = ["B0", "B1", "A0"][(j - 32) % 3]
                            pt_ = (j - 32) % 4
                        sEv.inc(s.activation(ptile(pt_)[:, lo:512], SCB[bk][:, lo:512],
                                             AF.Exp, bias=np.float32(ESHIFT),
                                             scale=np.float32(SCALE)))
                    elif op[0] == "evac":
                        _, H, h = op
                        av = ps_avA if h == 0 else ps_avB
                        sEv.inc(s.copy(aoch(H)[:, h * 512:(h + 1) * 512], av[:, :]))
                    elif op[0] == "ln":
                        sEv.inc(s.activation(lnrow, lrow, AF.Ln))
                    else:
                        sEv.inc(s.activation(lrow, lnrow, AF.Exp, scale=np.float32(-1.0)))

            @blk.gpsimd
            def _(g):
                wo_ready.append((sWa, sWa.inc(g.dma_start(out=wobuf(0), in_=wo.ap()[0]), 16)))

            @blk.vector
            def _(v):
                for op, w in zip(vplan, vwaits):
                    do_waits(v, w)
                    if op[0] == "mask":
                        _, H, j = op
                        c0 = (j // 8) * 128
                        d = j % 8
                        if j < 32:
                            pt_ = j % 2
                            lo = c0
                        else:
                            pt_ = (j - 32) % 4
                            lo = c0 - 512
                        sVe.inc(v.tensor_tensor(ptile(pt_)[:, lo:lo + 128],
                                                ptile(pt_)[:, lo:lo + 128],
                                                mask_sb[:, d * 128:(d + 1) * 128],
                                                ALU.mult))
                    elif op[0] == "lcopy":
                        _, H, h = op
                        lb = ps_lA if h == 0 else ps_lB
                        sVe.inc(v.tensor_copy(lrow, lb[0:1, :]))
                    else:
                        _, H, h = op
                        sVe.inc(v.tensor_tensor(aoch(H)[:, h * 512:(h + 1) * 512],
                                                aoch(H)[:, h * 512:(h + 1) * 512],
                                                ps_a1[:, :], ALU.mult))

            assert sMM.n == mm_end and sEv.n == ev_end and sVe.n == ve_end, \
                (sMM.n, mm_end, sEv.n, ev_end, sVe.n, ve_end)

        if "ao" in tap_t:
            with nc.Block("tap_ao") as blk:
                @blk.gpsimd
                def _(g):
                    dst = tap_t["ao"].ap().rearrange("(c p) t -> p c t", p=128)
                    src = T1.rearrange("p (c t) -> p c t", t=1024)
                    sW.inc(g.dma_start(out=dst, in_=src), 16)
                    g.wait_ge(sW.h, sW.n)

        with (
            nc.psum_tensor("ps_o0", [128, 512], F32) as ps_o0,
            nc.psum_tensor("ps_o1", [128, 512], F32) as ps_o1,
            nc.Block("wo") as blk,
        ):
            mm0, ev0, ve0 = sMM.n, sEv.n, sVe.n
            ps_o = [ps_o0, ps_o1]
            omm = [mm0 + it + 1 for it in range(64)]
            oadd = [ve0 + it + 1 for it in range(64)]
            ka0, kb0 = sKVa.n, sKVb.n
            g_ready = {it: ((sKVa, ka0 + 16 * (it // 2 + 1)) if it % 2 == 0
                            else (sKVb, kb0 + 16 * (it // 2 + 1))) for it in range(64)}

            def gslot(it):
                return p_sb[:, (it % 2) * 512:(it % 2) * 512 + 512]

            def xslot(it):
                return T4[:, (it % 2) * 512:(it % 2) * 512 + 512]

            def oslot(it):
                return rn_sb[:, (it % 2) * 512:(it % 2) * 512 + 512]

            gx_ready = []

            @blk.gpsimd
            def _(g):
                def x_load(it):
                    sp = sGXa if it % 2 == 0 else sGXb
                    if it >= 2:
                        g.wait_ge(sVe.h, oadd[it - 2])
                    sp.inc(g.dma_start(out=xslot(it), in_=x_tm.ap()[it]), 16)
                    gx_ready.append((sp, sp.n))

                x_load(0)
                x_load(1)
                for b in (1,):
                    wo_ready.append((sWb, sWb.inc(g.dma_start(out=wobuf(b), in_=wo.ap()[b]), 16)))
                for it in range(2, 10):
                    x_load(it)
                for b in range(2, 8):
                    g.wait_ge(sMM.h, omm[(b - 2) * 8 + 7])
                    sp = sWa if b % 2 == 0 else sWb
                    wo_ready.append((sp, sp.inc(g.dma_start(out=wobuf(b), in_=wo.ap()[b]), 16)))
                    for it in range((b - 2) * 8 + 10, (b - 2) * 8 + 18):
                        x_load(it)
                for it in range(58, 64):
                    x_load(it)
                g.wait_ge(sW.h, sW.n)

            @blk.tensor
            def _(t):
                for b in range(8):
                    t.wait_ge(wo_ready[b][0].h, wo_ready[b][1])
                    for ti in range(8):
                        it = b * 8 + ti
                        if it >= 2:
                            t.wait_ge(sVe.h, oadd[it - 2])
                        for j in range(DCH):
                            ins = t.matmul(ps_o[it % 2][:, :],
                                           T1[:, j * 1024 + ti * 128:j * 1024 + (ti + 1) * 128],
                                           wobuf(b)[:, j, :],
                                           start=(j == 0), stop=(j == DCH - 1))
                        sMM.inc(ins)

            @blk.vector
            def _(v):
                for it in range(64):
                    v.wait_ge(sMM.h, omm[it])
                    v.wait_ge(gx_ready[it][0].h, gx_ready[it][1])
                    v.wait_ge(g_ready[it][0].h, g_ready[it][1])
                    if it >= 2:
                        v.wait_ge(sOut.h, 16 * (it - 1))
                    v.tensor_tensor(xtmp_sb[:, 0:512], ps_o[it % 2][:, :], gslot(it), ALU.mult)
                    sVe.inc(v.tensor_tensor(oslot(it), xtmp_sb[:, 0:512], xslot(it), ALU.add))

            @blk.sync
            def _(sy):
                def g_load(it):
                    sp = sKVa if it % 2 == 0 else sKVb
                    assert g_ready[it][1] == sp.inc(
                        sy.dma_start(out=gslot(it), in_=g_dram.ap()[it]), 16)

                g_load(0)
                g_load(1)
                for it in range(64):
                    sy.wait_ge(sVe.h, oadd[it])
                    sy.wait_ge(sOut.h, 16 * it)
                    sOut.inc(sy.dma_start(out=out_t.ap()[it], in_=oslot(it)), 16)
                    if it + 2 < 64:
                        g_load(it + 2)
                sy.wait_ge(sOut.h, 16 * 64)

    return nc



def _llama3_inv_freq():
    base = 500000.0
    head_dim = 128
    inv_freq = 1.0 / base ** (np.arange(0, head_dim, 2, dtype=np.float64) / head_dim)
    factor, low_f, high_f, old_ctx = 32.0, 1.0, 4.0, 8192
    low_wl, high_wl = old_ctx / low_f, old_ctx / high_f
    wavelen = 2 * math.pi / inv_freq
    inv_llama = np.where(wavelen > low_wl, inv_freq / factor, inv_freq)
    smooth = (old_ctx / wavelen - low_f) / (high_f - low_f)
    smoothed = (1 - smooth) * inv_llama / factor + smooth * inv_llama
    is_med = ~(wavelen < high_wl) & ~(wavelen > low_wl)
    return np.where(is_med, smoothed, inv_llama).astype(np.float32)


_PERM = np.empty(128, dtype=np.int64)
_PERM[0::2] = np.arange(64)
_PERM[1::2] = np.arange(64) + 64


def prep_inputs(L0_attn_x, L0_attn_Wq, L0_attn_Wk, L0_attn_Wv, L0_attn_Wo,
                L0_ffn_lora_A, L0_ffn_lora_B, L0_ffn_W, L0_ffn_bias,
                L0_ffn_lora_weights):
    c = np.ascontiguousarray
    Wq = np.asarray(L0_attn_Wq, dtype=np.float32)
    Wk = np.asarray(L0_attn_Wk, dtype=np.float32)
    Wv = np.asarray(L0_attn_Wv, dtype=np.float32)
    Wo = np.asarray(L0_attn_Wo, dtype=np.float32)
    W = np.asarray(L0_ffn_W, dtype=np.float32)
    A = np.asarray(L0_ffn_lora_A, dtype=np.float32)
    Bm = np.asarray(L0_ffn_lora_B, dtype=np.float32)
    bias = np.asarray(L0_ffn_bias, dtype=np.float32)
    lw = np.asarray(L0_ffn_lora_weights, dtype=np.float32)
    x = np.asarray(L0_attn_x, dtype=np.float32)

    Wq_p = Wq.reshape(NQH, 128, D)[:, _PERM, :].reshape(D, D)
    Wk_p = Wk.reshape(NKVH, 128, D)[:, _PERM, :].reshape(NKVH * 128, D)
    wq_h = c(Wq_p.reshape(NQH, 128, DCH, 128).transpose(0, 3, 2, 1))
    wk_h = c(Wk_p.reshape(NKVH, 128, DCH, 128).transpose(0, 3, 2, 1))
    wv_h = c(Wv.reshape(NKVH, 128, DCH, 128).transpose(0, 3, 2, 1))
    wt_h = c(W.reshape(8, 512, DCH, 128).transpose(0, 3, 2, 1))
    wo_h = c(Wo.reshape(8, 512, DCH, 128).transpose(0, 3, 2, 1))
    at_h = c(A.transpose(2, 0, 1).reshape(D, 128).reshape(DCH, 128, 128).transpose(1, 0, 2))
    bt_h = c(Bm.transpose(0, 2, 1).reshape(128, D))
    bias_h = bias.reshape(1, D)
    lw_h = c(np.repeat(lw, 16).reshape(128, 1))

    inv = _llama3_inv_freq().astype(np.float64)
    xg = x.reshape(T, D)

    in_maps = []
    for i in range(N_CORES):
        xs = xg.reshape(NCH, 128, D)[i::8]
        xtm_full = xs.reshape(TPC, D)
        xtm = c(xtm_full.reshape(8, 128, 8, 512).transpose(2, 0, 1, 3).reshape(NCH, 128, 512))
        xfm = c(xtm_full.T)
        pos = np.concatenate(
            [np.arange((8 * m + i) * 128, (8 * m + i) * 128 + 128) for m in range(NSLOT)]
        ).astype(np.float64)
        ang = pos[None, :] * inv[:, None]
        cosv = np.cos(ang).astype(np.float32)
        sinv = np.sin(ang).astype(np.float32)
        cos_h = np.empty((128, TPC), dtype=np.float32)
        sin_h = np.empty((128, TPC), dtype=np.float32)
        cos_h[0::2] = cosv
        cos_h[1::2] = cosv
        sin_h[0::2] = -sinv
        sin_h[1::2] = sinv
        mask_h = np.zeros((128, NSLOT, 128), dtype=np.float32)
        kk = np.arange(128)[:, None]
        qq = np.arange(128)[None, :]
        for d in range(NSLOT):
            if d < i:
                mask_h[:, d, :] = 1.0
            elif d == i:
                mask_h[:, d, :] = (kk <= qq).astype(np.float32)
        in_maps.append({
            "x_fm": xfm, "x_tm": xtm, "wq": wq_h, "wk": wk_h, "wv": wv_h,
            "wt": wt_h, "wo": wo_h, "at": at_h, "bt": bt_h, "bias_in": bias_h,
            "lw_in": lw_h, "cos_in": cos_h, "sin_in": sin_h,
            "mask_in": c(mask_h),
        })
    return in_maps


_NC_CACHE = {}


def kernel(**inputs):
    in_maps = prep_inputs(**inputs)
    if "nc" not in _NC_CACHE:
        _NC_CACHE["nc"] = build()
    nc = _NC_CACHE["nc"]
    res = run_bass_kernel_spmd(nc, in_maps, core_ids=list(range(N_CORES)))
    full = np.empty((NCH, 128, D), dtype=np.float32)
    for i in range(N_CORES):
        o = res.results[i]["out"].reshape(8, 8, 128, 512).transpose(1, 2, 0, 3)
        full[i::8] = o.reshape(NSLOT, 128, D)
    return full.reshape(1, T, D)


# version 20
# speedup vs baseline: 1.0030x; 1.0030x over previous
import math
import numpy as np
from contextlib import ExitStack
from ml_dtypes import bfloat16

import concourse.bass as bass
import concourse.mybir as mybir
from concourse.bass_utils import run_bass_kernel_spmd

F32 = mybir.dt.float32
BF16 = mybir.dt.bfloat16
AF = mybir.ActivationFunctionType
ALU = mybir.AluOpType

N_CORES = 8
T = 8192
D = 4096
HD = 128
NQH = 32
NKVH = 8
TPC = 1024
NSLOT = 8
DCH = 32
NCH = 64
SCALE = 1.0 / math.sqrt(HD)
ESHIFT = -15.0
LN_EPS = 1e-5

SWAP_MASK = []
for _i in range(0, 32, 2):
    SWAP_MASK += [_i + 1, _i]


class Sem:
    def __init__(self, nc, es, name):
        self.h = es.enter_context(nc.semaphore(name))
        self.n = 0

    def inc(self, instr, k=1):
        instr.then_inc(self.h, k)
        self.n += k
        return self.n


HEAD_AT = list(range(32))


def build(taps=()):
    taps = set(taps)
    nc = bass.Bass("TRN2", target_bir_lowering=False)

    x_fm = nc.dram_tensor("x_fm", [D, TPC], F32, kind="ExternalInput")
    x_tm = nc.dram_tensor("x_tm", [NCH, 128, 512], F32, kind="ExternalInput")
    wq = nc.dram_tensor("wq", [NQH, 128, DCH, 128], F32, kind="ExternalInput")
    wk = nc.dram_tensor("wk", [NKVH, 128, DCH, 128], F32, kind="ExternalInput")
    wv = nc.dram_tensor("wv", [NKVH, 128, DCH, 128], F32, kind="ExternalInput")
    wt = nc.dram_tensor("wt", [8, 8, 128, 4, 512], F32, kind="ExternalInput")
    wo = nc.dram_tensor("wo", [8, 128, DCH, 512], F32, kind="ExternalInput")
    at = nc.dram_tensor("at", [128, DCH, 128], F32, kind="ExternalInput")
    bt = nc.dram_tensor("bt", [128, D], F32, kind="ExternalInput")
    bias_in = nc.dram_tensor("bias_in", [1, D], F32, kind="ExternalInput")
    lw_in = nc.dram_tensor("lw_in", [128, 1], F32, kind="ExternalInput")
    cos_in = nc.dram_tensor("cos_in", [128, TPC], F32, kind="ExternalInput")
    sin_in = nc.dram_tensor("sin_in", [128, TPC], F32, kind="ExternalInput")
    mask_in = nc.dram_tensor("mask_in", [128, NSLOT, 128], F32, kind="ExternalInput")

    out_t = nc.dram_tensor("out", [NCH, 128, 512], F32, kind="ExternalOutput")

    k_in = nc.dram_tensor("k_in", [NKVH * 128, TPC], BF16)
    v_in = nc.dram_tensor("v_in", [NKVH * TPC, 128], BF16)
    k_all = nc.dram_tensor("k_all", [N_CORES * NKVH * 128, TPC], BF16)
    v_all = nc.dram_tensor("v_all", [N_CORES * NKVH * TPC, 128], BF16)
    q_dram = nc.dram_tensor("q_dram", [D, TPC], BF16)
    g_dram = nc.dram_tensor("g_dram", [NCH, 128, 512], BF16)

    tap_t = {}
    tap_shapes = {
        "xn": [D, TPC], "kall": [N_CORES * NKVH * 128, TPC],
        "vall": [N_CORES * NKVH * TPC, 128],
        "q": [D, TPC], "ao": [D, TPC], "ffn3": [128, 16384], "g": [NCH, 128, 512],
        "wt01": [128, 32768],
    }
    for name in taps:
        tap_t[name] = nc.dram_tensor("tap_" + name, tap_shapes[name], F32,
                                     kind="ExternalOutput")

    es = ExitStack()
    with es:
        T1 = es.enter_context(nc.sbuf_tensor("T1", [128, 32768], BF16))
        T2 = es.enter_context(nc.sbuf_tensor("T2", [128, 32768], BF16))
        T3 = es.enter_context(nc.sbuf_tensor("T3", [128, 16384], BF16))
        T4 = es.enter_context(nc.sbuf_tensor("T4", [128, 4096], F32))
        cos_sb = es.enter_context(nc.sbuf_tensor("cos_sb", [128, TPC], BF16))
        sin_sb = es.enter_context(nc.sbuf_tensor("sin_sb", [128, TPC], BF16))
        mask_sb = es.enter_context(nc.sbuf_tensor("mask_sb", [128, NSLOT * 128], BF16))
        bias_sb = es.enter_context(nc.sbuf_tensor("bias_sb", [1, D], BF16))
        lw_sb = es.enter_context(nc.sbuf_tensor("lw_sb", [128, 1], F32))
        ones_col = es.enter_context(nc.sbuf_tensor("ones_col", [1, 128], BF16))
        ones_colf = es.enter_context(nc.sbuf_tensor("ones_colf", [1, 128], F32))
        ones_row = es.enter_context(nc.sbuf_tensor("ones_row", [128, 1], BF16))
        ident = es.enter_context(nc.sbuf_tensor("ident", [128, 128], BF16))
        rn_sb = es.enter_context(nc.sbuf_tensor("rn_sb", [128, TPC], F32))
        p_sb = es.enter_context(nc.sbuf_tensor("p_sb", [128, 2 * TPC], BF16))
        q_sb = es.enter_context(nc.sbuf_tensor("q_sb", [128, 2 * TPC], BF16))
        xtmp_sb = es.enter_context(nc.sbuf_tensor("xtmp_sb", [128, TPC], F32))

        sW = Sem(nc, es, "sW")
        sIO = Sem(nc, es, "sIO")
        sIO2 = Sem(nc, es, "sIO2")
        sWa = Sem(nc, es, "sWa")
        sWb = Sem(nc, es, "sWb")
        sVa = Sem(nc, es, "sVa")
        sVb = Sem(nc, es, "sVb")
        sKVa = Sem(nc, es, "sKVa")
        sKVb = Sem(nc, es, "sKVb")
        sQa = Sem(nc, es, "sQa")
        sQb = Sem(nc, es, "sQb")
        sGXa = Sem(nc, es, "sGXa")
        sGXb = Sem(nc, es, "sGXb")
        sOut = Sem(nc, es, "sOut")
        sMM = Sem(nc, es, "sMM")
        sEv = Sem(nc, es, "sEv")
        sVe = Sem(nc, es, "sVe")
        sPl = Sem(nc, es, "sPl")
        sCC = Sem(nc, es, "sCC")
        sCC2 = Sem(nc, es, "sCC2")
        sFSa = Sem(nc, es, "sFSa")
        sFSb = Sem(nc, es, "sFSb")
        sWT = Sem(nc, es, "sWT")
        sG = Sem(nc, es, "sG")

        btf = T4[:, 0:4096]
        bsc_sb = T3[:, 4096:8192]
        at_sb = T3[:, 8192:12288]
        h_sb = T3[:, 12288:13312]
        l_sb = T4[0:1, 0:TPC]
        lnl_sb = xtmp_sb

        def xch(j):
            return T1[:, j * 1024:(j + 1) * 1024]

        def xnch(j):
            return T2[:, j * 1024:(j + 1) * 1024]

        def aoch(h):
            return T1[:, h * 1024:(h + 1) * 1024]

        def pt(par):
            return p_sb[:, par * 1024:(par + 1) * 1024]

        def qt(par):
            return q_sb[:, par * 1024:(par + 1) * 1024]

        WBK = [0, 4096]
        WBV = [8192, 12288]

        def wkbuf(gi):
            return T3[:, WBK[gi % 2]:WBK[gi % 2] + 4096].rearrange("p (j n) -> p j n", n=128)

        def wvbuf(gi):
            return T3[:, WBV[gi % 2]:WBV[gi % 2] + 4096].rearrange("p (j n) -> p j n", n=128)

        WTB = [0, 16384]

        def wtbuf(b):
            return T1[:, WTB[b % 2]:WTB[b % 2] + 16384].rearrange("p (j n) -> p j n", n=512)

        def wqbuf(h):
            off = (h % 2) * 4096
            return T3[:, off:off + 4096].rearrange("p (j n) -> p j n", n=128)

        def wobuf(b):
            if b == 0:
                return T3[:, 0:16384].rearrange("p (j n) -> p j n", n=512)
            return T2[:, WTB[b % 2]:WTB[b % 2] + 16384].rearrange("p (j n) -> p j n", n=512)

        wk_ready = []
        wv_ready = []
        wt_ready = []
        wq_ready = []
        wo_ready = []
        at_ready = []
        castev = {}

        xcast = {f: ((sWT, sWT.n + f // 2 + 1) if f % 2 == 0
                     else (sVe, sVe.n + (f - 1) // 2 + 1)) for f in range(16)}
        xn_ready = [None] * DCH
        m_v = T4[0:1, 0:TPC]
        sq_v = T4[0:1, TPC:2 * TPC]
        rstd_v = T4[0:1, 2 * TPC:3 * TPC]
        negmr_v = T4[0:1, 3 * TPC:4 * TPC]
        msq_v = T4[0:1, 3 * TPC:4 * TPC]
        es12 = ExitStack()
        ps_k = es12.enter_context(nc.psum_tensor("ps_k", [128, TPC], F32))
        ps_b1 = ps_k
        with (
            nc.psum_tensor("ps_sx", [1, TPC], F32) as ps_sx,
            nc.psum_tensor("ps_sq", [1, TPC], F32) as ps_sq,
            nc.psum_tensor("ps_b2", [128, TPC], F32) as ps_b2,
            nc.Block("ln") as blk,
        ):
            mm0, ev0 = sMM.n, sEv.n
            vbase = sVe.n
            ve0 = vbase + 16
            pl0 = sPl.n
            frac_dma = []
            for j in range(DCH):
                xn_ready[j] = (sVe, ve0 + 4 + j + 1)
            sq_ready = [(sEv, ev0 + j + 1) if j < 24 else (sVe, vbase + 8 + (j - 24) + 1)
                        for j in range(DCH)]

            @blk.gpsimd
            def _(g):
                sW.inc(g.dma_start(out=cos_sb[:, :], in_=cos_in.ap()), 16)
                sW.inc(g.dma_start(out=sin_sb[:, :], in_=sin_in.ap()), 16)
                sW.inc(g.dma_start(out=mask_sb.rearrange("p (d q) -> p d q", q=128),
                                   in_=mask_in.ap()), 16)
                sW.inc(g.dma_start(out=bias_sb[:, :], in_=bias_in.ap()), 16)
                sW.inc(g.dma_start(out=lw_sb[:, :], in_=lw_in.ap()), 16)
                wk_ready.append((sWa, sWa.inc(g.dma_start(out=wkbuf(0), in_=wk.ap()[0]), 16)))
                wv_ready.append((sVa, sVa.inc(g.dma_start(out=wvbuf(0), in_=wv.ap()[0]), 16)))
                wk_ready.append((sWb, sWb.inc(g.dma_start(out=wkbuf(1), in_=wk.ap()[1]), 16)))
                wv_ready.append((sVb, sVb.inc(g.dma_start(out=wvbuf(1), in_=wv.ap()[1]), 16)))
                g.memset(ones_col[:, :], 1.0)
                g.memset(ones_colf[:, :], 1.0)
                g.memset(ones_row[:, :], 1.0)
                g.memset(ident[:, :], 0.0)
                sPl.inc(g.affine_select(out=ident[:, :], in_=ident[:, :],
                                        compare_op=ALU.not_equal, fill=1.0, base=0,
                                        pattern=[[-1, 128]], channel_multiplier=1))
                g.wait_ge(sW.h, sW.n)

            @blk.sync
            def _(sy):
                for f in range(16):
                    sp = sFSa if f % 2 == 0 else sFSb
                    if f >= 2:
                        sy.wait_ge(xcast[f - 2][0].h, xcast[f - 2][1])
                    srcx = x_fm.ap().rearrange("(c p) t -> p c t", p=128)[:, 2 * f:2 * f + 2, :]
                    dst = T4[:, (f % 2) * 2048:(f % 2) * 2048 + 2048].rearrange(
                        "p (c t) -> p c t", t=1024)
                    frac_dma.append((sp, sp.inc(sy.dma_start(out=dst, in_=srcx), 16)))

            @blk.scalar
            def _(s):
                for f in range(16):
                    if f % 2 == 0:
                        s.wait_ge(frac_dma[f][0].h, frac_dma[f][1])
                        assert xcast[f][1] == sWT.inc(s.copy(
                            T1[:, f * 2048:(f + 1) * 2048], T4[:, 0:2048]))
                    elif f < 12:
                        s.wait_ge(xcast[f][0].h, xcast[f][1])
                    if f < 12:
                        for j in (2 * f, 2 * f + 1):
                            if j >= 2:
                                s.wait_ge(sMM.h, mm0 + j - 1)
                            sEv.inc(s.activation(qt(j % 2), xch(j), AF.Square))
                s.wait_ge(sVe.h, ve0 + 1)
                s.activation(rstd_v, sq_v, AF.Ln, bias=np.float32(LN_EPS))
                sEv.inc(s.activation(rstd_v, rstd_v, AF.Exp, scale=np.float32(-0.5)))

            @blk.tensor
            def _(t):
                t.wait_ge(sPl.h, pl0 + 1)
                for j in range(DCH):
                    if j % 2 == 0:
                        t.wait_ge(xcast[j // 2][0].h, xcast[j // 2][1])
                    t.matmul(ps_sx[:, 0:512], ones_row[:, :], xch(j)[:, 0:512],
                             start=(j == 0), stop=(j == DCH - 1))
                    t.matmul(ps_sx[:, 512:1024], ones_row[:, :], xch(j)[:, 512:1024],
                             start=(j == 0), stop=(j == DCH - 1))
                    t.wait_ge(sq_ready[j][0].h, sq_ready[j][1])
                    sq_src = qt(j % 2)
                    t.matmul(ps_sq[:, 0:512], ones_row[:, :], sq_src[:, 0:512],
                             start=(j == 0), stop=(j == DCH - 1))
                    sMM.inc(t.matmul(ps_sq[:, 512:1024], ones_row[:, :], sq_src[:, 512:1024],
                                     start=(j == 0), stop=(j == DCH - 1)))
                t.wait_ge(sVe.h, ve0 + 2)
                t.matmul(ps_b1[:, 0:512], ones_colf[:, :], rstd_v[:, 0:512], start=True, stop=True)
                t.matmul(ps_b1[:, 512:1024], ones_colf[:, :], rstd_v[:, 512:1024], start=True, stop=True)
                t.matmul(ps_b2[:, 0:512], ones_colf[:, :], negmr_v[:, 0:512], start=True, stop=True)
                sMM.inc(t.matmul(ps_b2[:, 512:1024], ones_colf[:, :], negmr_v[:, 512:1024],
                                 start=True, stop=True))
                t.wait_ge(wk_ready[0][0].h, wk_ready[0][1])
                for j in range(DCH):
                    t.wait_ge(xn_ready[j][0].h, xn_ready[j][1])
                    for half in range(2):
                        ins = t.matmul(ps_k[:, half * 512:half * 512 + 512],
                                       wkbuf(0)[:, j, :],
                                       xnch(j)[:, half * 512:half * 512 + 512],
                                       start=(j == 0), stop=(j == DCH - 1))
                sMM.inc(ins)

            @blk.vector
            def _(v):
                for f in range(1, 16, 2):
                    v.wait_ge(frac_dma[f][0].h, frac_dma[f][1])
                    assert xcast[f][1] == sVe.inc(v.tensor_copy(
                        T1[:, f * 2048:(f + 1) * 2048], T4[:, 2048:4096]))
                for j in range(24, DCH):
                    if j % 2 == 0:
                        v.wait_ge(xcast[j // 2][0].h, xcast[j // 2][1])
                    v.wait_ge(sMM.h, mm0 + j - 1)
                    assert sq_ready[j][1] == sVe.inc(
                        v.tensor_tensor(qt(j % 2), xch(j), xch(j), ALU.mult))
                v.wait_ge(sMM.h, mm0 + DCH)
                v.tensor_scalar_mul(m_v, ps_sx[:, :], 1.0 / D)
                v.tensor_scalar_mul(sq_v, ps_sq[:, :], 1.0 / D)
                v.tensor_tensor(msq_v, m_v, m_v, ALU.mult)
                sVe.inc(v.tensor_tensor(sq_v, sq_v, msq_v, ALU.subtract))
                v.wait_ge(sEv.h, ev0 + 25)
                sVe.inc(v.scalar_tensor_tensor(negmr_v, m_v, -1.0, rstd_v, ALU.mult, ALU.mult))
                v.wait_ge(sMM.h, mm0 + DCH + 1)
                sVe.inc(v.tensor_copy(p_sb[:, 0:TPC], ps_b1[:, :]))
                sVe.inc(v.tensor_copy(p_sb[:, TPC:2 * TPC], ps_b2[:, :]))
                for j in range(DCH):
                    v.tensor_tensor(q_sb[:, 0:1024], xch(j), p_sb[:, 0:TPC], ALU.mult)
                    sVe.inc(v.tensor_tensor(xnch(j), q_sb[:, 0:1024], p_sb[:, TPC:2 * TPC], ALU.add))


        if "xn" in tap_t:
            with nc.Block("tap_xn") as blk:
                @blk.gpsimd
                def _(g):
                    dst = tap_t["xn"].ap().rearrange("(c p) t -> p c t", p=128)
                    srcx = T2.rearrange("p (c t) -> p c t", t=1024)
                    sW.inc(g.dma_start(out=dst, in_=srcx), 16)
                    g.wait_ge(sW.h, sW.n)

        with (
            nc.psum_tensor("ps_v", [128, TPC], F32) as ps_v,
            nc.psum_tensor("ps_tr0", [128, 128], BF16) as ps_tr0,
            nc.psum_tensor("ps_tr1", [128, 128], BF16) as ps_tr1,
            nc.Block("kv") as blk,
        ):
            mm0, ev0, ve0, w0, io0 = sMM.n, sEv.n, sVe.n, sW.n, sIO.n
            ps_tr = [ps_tr0, ps_tr1]
            kmm = [mm0 + gi for gi in range(NKVH)]
            vmm = [mm0 + 7 + 9 * gi + 1 for gi in range(NKVH)]
            vtr = [[mm0 + 7 + 9 * gi + 2 + c for c in range(8)] for gi in range(NKVH)]
            kev = [ev0 + gi + 1 for gi in range(NKVH)]
            vev = [ev0 + 8 + gi + 1 for gi in range(NKVH)]
            kve = [ve0 + gi + 1 for gi in range(NKVH)]
            vtc = [[ve0 + 8 + 8 * gi + c + 1 for c in range(8)] for gi in range(NKVH)]
            kdma = [io0 + 16 * (gi + 1) for gi in range(NKVH)]
            vdma = [io0 + 128 + 16 * (gi + 1) for gi in range(NKVH)]
            wt0 = sWT.n
            castev.update({k: (sWT, wt0 + k + 1) for k in range(16)})
            veP1, plP1 = sVe.n, sPl.n
            fsa0, fsb0 = sFSa.n, sFSb.n
            fracv = {k: ((sFSa, fsa0 + 16 * (k // 2 + 1)) if k % 2 == 0
                         else (sFSb, fsb0 + 16 * (k // 2 + 1))) for k in range(16)}

            @blk.gpsimd
            def _(g):
                for gi in range(2, NKVH):
                    g.wait_ge(sMM.h, kmm[gi - 2])
                    sp = sWa if gi % 2 == 0 else sWb
                    wk_ready.append((sp, sp.inc(g.dma_start(out=wkbuf(gi), in_=wk.ap()[gi]), 16)))
                for gi in range(2, NKVH):
                    g.wait_ge(sMM.h, vmm[gi - 2])
                    sp = sVa if gi % 2 == 0 else sVb
                    wv_ready.append((sp, sp.inc(g.dma_start(out=wvbuf(gi), in_=wv.ap()[gi]), 16)))
                g.wait_ge(sMM.h, vmm[6])
                at_ready.append((sW, sW.inc(g.dma_start(
                    out=at_sb.rearrange("p (j n) -> p j n", n=128), in_=at.ap()), 16)))
                g.wait_ge(sIO.h, io0 + 256)
                sCC.inc(g.collective_compute(
                    "AllGather", ALU.bypass, replica_groups=[list(range(N_CORES))],
                    ins=[k_in.ap().opt()], outs=[k_all.ap().opt()]))
                sCC2.inc(g.collective_compute(
                    "AllGather", ALU.bypass, replica_groups=[list(range(N_CORES))],
                    ins=[v_in.ap().opt()], outs=[v_all.ap().opt()]))

            @blk.tensor
            def _(t):
                for gi in range(1, NKVH):
                    t.wait_ge(wk_ready[gi][0].h, wk_ready[gi][1])
                    t.wait_ge(sEv.h, kev[gi - 1])
                    for j in range(DCH):
                        for half in range(2):
                            ins = t.matmul(ps_k[:, half * 512:half * 512 + 512],
                                           wkbuf(gi)[:, j, :],
                                           xnch(j)[:, half * 512:half * 512 + 512],
                                           start=(j == 0), stop=(j == DCH - 1))
                    sMM.inc(ins)
                for gi in range(NKVH):
                    t.wait_ge(wv_ready[gi][0].h, wv_ready[gi][1])
                    if gi >= 1:
                        t.wait_ge(sEv.h, vev[gi - 1])
                    for j in range(DCH):
                        for half in range(2):
                            ins = t.matmul(ps_v[:, half * 512:half * 512 + 512],
                                           wvbuf(gi)[:, j, :],
                                           xnch(j)[:, half * 512:half * 512 + 512],
                                           start=(j == 0), stop=(j == DCH - 1))
                    sMM.inc(ins)
                    t.wait_ge(sEv.h, vev[gi])
                    for c in range(8):
                        if c >= 2:
                            t.wait_ge(sVe.h, vtc[gi][c - 2])
                        elif gi >= 1 and c < 2:
                            t.wait_ge(sVe.h, vtc[gi - 1][c + 6])
                        sMM.inc(t.transpose(ps_tr[c % 2][:, :],
                                            pt(gi % 2)[:, c * 128:(c + 1) * 128],
                                            ident[:, :]))

            @blk.scalar
            def _(s):
                for gi in range(NKVH):
                    s.wait_ge(sMM.h, kmm[gi])
                    if gi >= 2:
                        s.wait_ge(sIO.h, kdma[gi - 2])
                    sEv.inc(s.copy(pt(gi % 2), ps_k[:, :]))
                    for k in (2 * gi, 2 * gi + 1):
                        b, f = divmod(k, 8)
                        s.wait_ge(fracv[k][0].h, fracv[k][1])
                        stg = T4[:, (k % 2) * 2048:(k % 2) * 2048 + 2048].rearrange(
                            "p (j n) -> p j n", n=512)
                        assert castev[k][1] == sWT.inc(s.copy(
                            wtbuf(b)[:, 4 * f:4 * f + 4, :], stg))
                for gi in range(NKVH):
                    s.wait_ge(sMM.h, vmm[gi])
                    if gi == 0:
                        s.wait_ge(sIO.h, kdma[6])
                    elif gi == 1:
                        s.wait_ge(sIO.h, kdma[7])
                    else:
                        s.wait_ge(sIO.h, vdma[gi - 2])
                    sEv.inc(s.copy(pt(gi % 2), ps_v[:, :]))

            @blk.vector
            def _(v):
                for gi in range(NKVH):
                    v.wait_ge(sEv.h, kev[gi])
                    kraw = pt(gi % 2)
                    krot = qt(gi % 2)
                    v.stream_shuffle(krot, kraw, SWAP_MASK)
                    v.tensor_tensor(xtmp_sb[:, :], kraw, cos_sb[:, :], ALU.mult)
                    v.tensor_tensor(rn_sb[:, :], krot, sin_sb[:, :], ALU.mult)
                    sVe.inc(v.tensor_tensor(kraw, xtmp_sb[:, :], rn_sb[:, :], ALU.add))
                for gi in range(NKVH):
                    if gi == 0:
                        v.wait_ge(sIO.h, kdma[6])
                    elif gi == 1:
                        v.wait_ge(sIO.h, kdma[7])
                    else:
                        v.wait_ge(sIO.h, vdma[gi - 2])
                    for c in range(8):
                        v.wait_ge(sMM.h, vtr[gi][c])
                        sVe.inc(v.tensor_copy(qt(gi % 2)[:, c * 128:(c + 1) * 128],
                                              ps_tr[c % 2][:, :]))

            @blk.sync
            def _(sy):
                def frac_issue(k):
                    b, f = divmod(k, 8)
                    sp = sFSa if k % 2 == 0 else sFSb
                    if k >= 2:
                        sy.wait_ge(castev[k - 2][0].h, castev[k - 2][1])
                    stg = T4[:, (k % 2) * 2048:(k % 2) * 2048 + 2048].rearrange(
                        "p (j n) -> p j n", n=512)
                    assert fracv[k][1] == sp.inc(sy.dma_start(
                        out=stg, in_=wt.ap()[b][f]), 16)

                sy.wait_ge(sVe.h, veP1)
                sy.wait_ge(sPl.h, plP1)
                frac_issue(0)
                frac_issue(1)
                for gi in range(NKVH):
                    sy.wait_ge(sVe.h, kve[gi])
                    sy.wait_ge(sIO.h, sIO.n)
                    sIO.inc(sy.dma_start(out=k_in.ap()[gi * 128:(gi + 1) * 128, :],
                                         in_=pt(gi % 2)), 16)
                    if gi < 7:
                        frac_issue(2 * gi + 2)
                        frac_issue(2 * gi + 3)
                for gi in range(NKVH):
                    sy.wait_ge(sVe.h, vtc[gi][7])
                    sy.wait_ge(sIO.h, sIO.n)
                    dst = v_in.ap()[gi * 1024:(gi + 1) * 1024, :].rearrange(
                        "(c p) h -> p c h", p=128)
                    src = qt(gi % 2).rearrange("p (c h) -> p c h", h=128)
                    sIO.inc(sy.dma_start(out=dst, in_=src), 16)
                sy.wait_ge(sIO.h, sIO.n)

        if "wt01" in tap_t:
            with nc.Block("tap_wt01") as blk:
                @blk.gpsimd
                def _(g):
                    sW.inc(g.dma_start(out=tap_t["wt01"].ap(), in_=T1[:, :]), 16)
                    g.wait_ge(sW.h, sW.n)

        es12.close()

        with (
            nc.psum_tensor("ps_h", [128, TPC], F32) as ps_h,
            nc.psum_tensor("ps_f0", [128, 512], F32) as ps_f0,
            nc.psum_tensor("ps_f1", [128, 512], F32) as ps_f1,
            nc.Block("ffn") as blk,
        ):
            mm0, ev0, ve0, w0, g0 = sMM.n, sEv.n, sVe.n, sW.n, sG.n
            ps_f = [ps_f0, ps_f1]
            fmm = [mm0 + 2 + it + 1 for it in range(64)]
            fsil = [ev0 + 1 + it + 1 for it in range(64)]

            bt_ev = []
            frac = []
            castev.update({k: (sVe, sVe.n + 2 + (k - 16)) for k in range(16, 64)})
            for b in range(8):
                wt_ready.append(castev[b * 8 + 7])

            @blk.gpsimd
            def _(g):
                def g_store(it):
                    g.wait_ge(sEv.h, fsil[it])
                    g.wait_ge(sG.h, sG.n)
                    sG.inc(g.dma_start(
                        out=g_dram.ap()[it],
                        in_=p_sb[:, (it % 4) * 512:(it % 4) * 512 + 512]), 16)

                for it in range(8):
                    g_store(it)
                wq_ready.append((sVa, sVa.inc(g.dma_start(out=wqbuf(0), in_=wq.ap()[0]), 16)))
                for it in range(8, 64):
                    g_store(it)
                g.wait_ge(sG.h, sG.n)
                g.wait_ge(sW.h, sW.n)

            @blk.sync
            def _(sy):
                def frac_issue(k):
                    b, f = divmod(k, 8)
                    sp = sFSa if k % 2 == 0 else sFSb
                    sy.wait_ge(castev[k - 2][0].h, castev[k - 2][1])
                    stg = T4[:, (k % 2) * 2048:(k % 2) * 2048 + 2048].rearrange(
                        "p (j n) -> p j n", n=512)
                    frac.append((sp, sp.inc(sy.dma_start(
                        out=stg, in_=wt.ap()[b][f]), 16)))

                bt_ev.append((sFSa, sFSa.inc(sy.dma_start(out=btf, in_=bt.ap()), 16)))
                sy.wait_ge(sVe.h, ve0 + 1)
                for k in range(16, 64):
                    frac_issue(k)

            @blk.tensor
            def _(t):
                t.wait_ge(at_ready[0][0].h, at_ready[0][1])
                for j in range(DCH):
                    for half in range(2):
                        ins = t.matmul(ps_h[:, half * 512:half * 512 + 512],
                                       at_sb[:, j * 128:(j + 1) * 128],
                                       xnch(j)[:, half * 512:half * 512 + 512],
                                       start=(j == 0), stop=(j == DCH - 1))
                        if j == DCH - 1:
                            sMM.inc(ins)
                t.wait_ge(sEv.h, ev0 + 1)
                t.wait_ge(sVe.h, ve0 + 1)
                for b in range(8):
                    t.wait_ge(wt_ready[b][0].h, wt_ready[b][1])
                    for ti in range(8):
                        it = b * 8 + ti
                        if it >= 2:
                            t.wait_ge(sEv.h, fsil[it - 2])
                        for j in range(DCH):
                            t.matmul(ps_f[it % 2][:, :],
                                     xnch(j)[:, ti * 128:(ti + 1) * 128],
                                     wtbuf(b)[:, j, :],
                                     start=(j == 0), stop=False)
                        t.matmul(ps_f[it % 2][:, :],
                                 h_sb[:, ti * 128:(ti + 1) * 128],
                                 bsc_sb[:, b * 512:(b + 1) * 512],
                                 start=False, stop=False)
                        sMM.inc(t.matmul(ps_f[it % 2][:, :],
                                         ones_col[:, :],
                                         bias_sb[:, b * 512:(b + 1) * 512],
                                         start=False, stop=True))

            @blk.vector
            def _(v):
                v.wait_ge(bt_ev[0][0].h, bt_ev[0][1])
                sVe.inc(v.tensor_scalar(bsc_sb[:, :], btf, lw_sb[:, :],
                                        2.0, ALU.mult, ALU.mult))
                for k in range(16, 64):
                    b, f = divmod(k, 8)
                    v.wait_ge(frac[k - 16][0].h, frac[k - 16][1])
                    if f == 0 and b >= 2:
                        v.wait_ge(sMM.h, fmm[(b - 2) * 8 + 7])
                    stg = T4[:, (k % 2) * 2048:(k % 2) * 2048 + 2048].rearrange(
                        "p (j n) -> p j n", n=512)
                    assert castev[k][1] == sVe.inc(v.tensor_copy(
                        wtbuf(b)[:, 4 * f:4 * f + 4, :], stg))

            @blk.scalar
            def _(s):
                s.wait_ge(sMM.h, mm0 + 2)
                sEv.inc(s.copy(h_sb[:, :], ps_h[:, :]))
                for b in range(8):
                    for ti in range(8):
                        it = b * 8 + ti
                        s.wait_ge(sMM.h, fmm[it])
                        slot = p_sb[:, (it % 4) * 512:(it % 4) * 512 + 512]
                        if it >= 4:
                            s.wait_ge(sG.h, g0 + 16 * (it - 3))
                        sEv.inc(s.activation(slot, ps_f[it % 2][:, :], AF.Silu))

            if "g" in tap_t:
                @blk.sync
                def _(sy):
                    sy.wait_ge(sEv.h, sEv.n)

        if "ffn3" in tap_t:
            with nc.Block("tap_ffn3") as blk:
                @blk.gpsimd
                def _(g):
                    sW.inc(g.dma_start(out=tap_t["ffn3"].ap(), in_=T3[:, :]), 16)
                    g.wait_ge(sW.h, sW.n)

        if "g" in tap_t:
            with nc.Block("tap_g") as blk:
                @blk.gpsimd
                def _(g):
                    sW.inc(g.dma_start(out=tap_t["g"].ap(), in_=g_dram.ap()), 16)
                    g.wait_ge(sW.h, sW.n)

        with (
            nc.psum_tensor("ps_q", [128, TPC], F32) as ps_q,
            nc.Block("qproj") as blk,
        ):
            mm0, ev0, ve0, w0, io0 = sMM.n, sEv.n, sVe.n, sW.n, sIO.n
            qmm = [mm0 + h + 1 for h in range(NQH)]
            qev = [ev0 + h + 1 for h in range(NQH)]
            qve = [ve0 + h + 1 for h in range(NQH)]
            qdma = [io0 + 16 * (h + 1) for h in range(NQH)]

            @blk.gpsimd
            def _(g):
                for h in (1,):
                    wq_ready.append((sVb, sVb.inc(g.dma_start(out=wqbuf(h), in_=wq.ap()[h]), 16)))
                for h in range(2, NQH):
                    g.wait_ge(sMM.h, qmm[h - 2])
                    sp = sVa if h % 2 == 0 else sVb
                    wq_ready.append((sp, sp.inc(g.dma_start(out=wqbuf(h), in_=wq.ap()[h]), 16)))

            @blk.tensor
            def _(t):
                for h in range(NQH):
                    t.wait_ge(wq_ready[h][0].h, wq_ready[h][1])
                    if h >= 1:
                        t.wait_ge(sEv.h, qev[h - 1])
                    for j in range(DCH):
                        for half in range(2):
                            ins = t.matmul(ps_q[:, half * 512:half * 512 + 512],
                                           wqbuf(h)[:, j, :],
                                           xnch(j)[:, half * 512:half * 512 + 512],
                                           start=(j == 0), stop=(j == DCH - 1))
                    sMM.inc(ins)

            @blk.scalar
            def _(s):
                for h in range(NQH):
                    s.wait_ge(sMM.h, qmm[h])
                    if h >= 2:
                        s.wait_ge(sIO.h, qdma[h - 2])
                    sEv.inc(s.copy(pt(h % 2), ps_q[:, :]))

            @blk.vector
            def _(v):
                for h in range(NQH):
                    v.wait_ge(sEv.h, qev[h])
                    qraw = pt(h % 2)
                    qrot = qt(h % 2)
                    v.stream_shuffle(qrot, qraw, SWAP_MASK)
                    v.tensor_tensor(xtmp_sb[:, :], qraw, cos_sb[:, :], ALU.mult)
                    v.tensor_tensor(rn_sb[:, :], qrot, sin_sb[:, :], ALU.mult)
                    sVe.inc(v.tensor_tensor(qraw, xtmp_sb[:, :], rn_sb[:, :], ALU.add))

            @blk.sync
            def _(sy):
                sy.wait_ge(sCC.h, 1)
                sy.wait_ge(sCC2.h, 1)
                for r in range(N_CORES):
                    srck = k_all.ap()[r * 1024:r * 1024 + 128, :] \
                        .rearrange("q (m t) -> q m t", t=128)
                    dstk = T1[:, 16384:24576].rearrange("p (m rr t) -> p m rr t", rr=8, t=128)[:, :, r, :]
                    sPl.inc(sy.dma_start(out=dstk, in_=srck), 16)
                    srcv = v_all.ap()[r * 8192:r * 8192 + 1024, :] \
                        .rearrange("(m p) h -> p m h", p=128)
                    dstv = T1[:, 24576:32768].rearrange("p (m rr t) -> p m rr t", rr=8, t=128)[:, :, r, :]
                    sPl.inc(sy.dma_start(out=dstv, in_=srcv), 16)
                for h in range(NQH):
                    sy.wait_ge(sVe.h, qve[h])
                    sy.wait_ge(sIO.h, sIO.n)
                    sIO.inc(sy.dma_start(out=q_dram.ap()[h * 128:(h + 1) * 128, :],
                                         in_=pt(h % 2)), 16)
                sy.wait_ge(sIO.h, sIO.n)

        for name, src_d in (("q", q_dram), ("kall", k_all), ("vall", v_all)):
            if name in tap_t:
                with nc.Block("tap_" + name) as blk:
                    @blk.gpsimd
                    def _(g, src_d=src_d, name=name):
                        if name in ("kall", "vall"):
                            g.wait_ge(sCC.h, 1)
                            g.wait_ge(sCC2.h, 1)
                        sW.inc(g.dma_start(out=tap_t[name].ap(), in_=src_d.ap()), 16)
                        g.wait_ge(sW.h, sW.n)

        def KG(g):
            if g == 0:
                return T1[:, 16384:24576]
            b = (g - 1) % 2
            return T2[:, b * 16384:b * 16384 + 8192]

        def VG(g):
            if g == 0:
                return T1[:, 24576:32768]
            b = (g - 1) % 2
            return T2[:, b * 16384 + 8192:(b + 1) * 16384]

        with (
            nc.psum_tensor("ps_at_a0", [128, 512], F32) as ps_a0,
            nc.psum_tensor("ps_at_a1", [128, 512], F32) as ps_a1,
            nc.psum_tensor("ps_at_b0", [128, 512], F32) as ps_b0,
            nc.psum_tensor("ps_at_b1", [128, 512], F32) as ps_b1,
            nc.psum_tensor("ps_avA", [128, 512], F32) as ps_avA,
            nc.psum_tensor("ps_avB", [128, 512], F32) as ps_avB,
            nc.psum_tensor("ps_lA", [1, 512], F32) as ps_lA,
            nc.psum_tensor("ps_lB", [1, 512], F32) as ps_lB,
            nc.Block("attn") as blk,
        ):
            NH = NQH
            SCB = {"A0": ps_a0, "A1": ps_a1, "B0": ps_b0, "B1": ps_b1}

            def ptile(r):
                return p_sb[:, r * 512:(r + 1) * 512]

            lrow = l_sb[0:1, 0:512]
            lnrow = l_sb[0:1, 512:1024]

            tplan, splan, vplan = [], [], []
            for H in range(NH):
                for step in range(67):
                    if step == 38 and H >= 1:
                        tplan.append(("bcast", H - 1, 1))
                    if step == 50:
                        tplan.append(("bcast", H, 0))
                    if step <= 63:
                        j = step
                        if j < 32:
                            tplan.append(("scores", H, j, 0, "A0" if j % 2 == 0 else "A1"))
                            tplan.append(("scores", H, j, 1, "B0" if j % 2 == 0 else "B1"))
                        else:
                            tplan.append(("scores", H, j, 1, ["B0", "B1", "A0"][(j - 32) % 3]))
                    jj = step - 2
                    if 0 <= jj < 32:
                        tplan.append(("avsum", H, jj, 1, 2 + jj % 2))
                        tplan.append(("avsum", H, jj, 0, jj % 2))
                    jj = step - 3
                    if 32 <= jj <= 63:
                        tplan.append(("avsum", H, jj, 1, (jj - 32) % 4))
            tplan.append(("bcast", NH - 1, 1))

            for H in range(NH):
                if H >= 1:
                    splan.append(("evac", H - 1, 1))
                for j in range(64):
                    if j == 2 and H >= 1:
                        splan.append(("ln", H - 1, 1))
                    if j == 3 and H >= 1:
                        splan.append(("rn", H - 1, 1))
                    if j == 36:
                        splan.append(("evac", H, 0))
                    if j == 44:
                        splan.append(("ln", H, 0))
                    if j == 46:
                        splan.append(("rn", H, 0))
                    if j < 32:
                        splan.append(("exp", H, j, 0))
                    splan.append(("exp", H, j, 1))
            splan += [("evac", NH - 1, 1), ("ln", NH - 1, 1), ("rn", NH - 1, 1)]

            for H in range(NH):
                if H >= 1:
                    vplan.append(("lcopy", H - 1, 1))
                for j in range(64):
                    if j == 38:
                        vplan.append(("lcopy", H, 0))
                    if j == 40 and H >= 1:
                        vplan.append(("mult", H - 1, 1))
                    if j == 52:
                        vplan.append(("mult", H, 0))
                    vplan.append(("mask", H, j))
            vplan += [("lcopy", NH - 1, 1), ("mult", NH - 1, 1)]

            mm0, ev0, ve0 = sMM.n, sEv.n, sVe.n
            SCc = {}; ASc = {}; BCc = {}; AS0last = {}; AS1last = {}; SC1last = {}
            c = mm0
            for op in tplan:
                c += 1
                if op[0] == "scores":
                    _, H, j, h, bk = op
                    SCc[(H, j, h)] = c
                    if h == 1 and j == 63:
                        SC1last[H] = c
                elif op[0] == "avsum":
                    _, H, j, h, pt_ = op
                    ASc[(H, j, h)] = c
                    if h == 0 and j == 31:
                        AS0last[H] = c
                    if h == 1 and j == 63:
                        AS1last[H] = c
                else:
                    _, w, h = op
                    BCc[(w, h)] = c
            mm_end = c
            EXc = {}; EVAC = {}; RNc = {}
            c = ev0
            for op in splan:
                c += 1
                if op[0] == "exp":
                    EXc[(op[1], op[2], op[3])] = c
                elif op[0] == "evac":
                    EVAC[(op[1], op[2])] = c
                elif op[0] == "rn":
                    RNc[(op[1], op[2])] = c
            ev_end = c
            MKc = {}; LCc = {}; MUc = {}
            c = ve0
            for op in vplan:
                c += 1
                if op[0] == "mask":
                    MKc[(op[1], op[2])] = c
                elif op[0] == "lcopy":
                    LCc[(op[1], op[2])] = c
                else:
                    MUc[(op[1], op[2])] = c
            ve_end = c

            twaits = []
            bank_pend = {k: [] for k in ("A0", "A1", "B0", "B1")}
            pt_prev = {}
            exp_war = {}
            for op in tplan:
                w = []
                if op[0] == "scores":
                    _, H, j, h, bk = op
                    w += bank_pend[bk]
                    bank_pend[bk] = [("sEv", EXc[(H, j, h)])]
                    if (H, j) == (0, 0) and h == 0:
                        pass
                elif op[0] == "avsum":
                    _, H, j, h, pt_ = op
                    if h == 0 or j >= 32:
                        w.append(("sVe", MKc[(H, j)]))
                    else:
                        w.append(("sEv", EXc[(H, j, 1)]))
                    if j == 0 and H >= 1:
                        w.append(("sEv", EVAC[(H - 1, h)]))
                        w.append(("sVe", LCc[(H - 1, h)]))
                    exp_war[(H, j, h)] = pt_prev.get(pt_)
                    pt_prev[pt_] = ("sMM", ASc[(H, j, h)])
                else:
                    _, wd, h = op
                    w.append(("sEv", RNc[(wd, h)]))
                    w += bank_pend["A1"]
                    bank_pend["A1"] = [("sVe", MUc[(wd, h)])]
                twaits.append(w)
            swaits = []
            for op in splan:
                w = []
                if op[0] == "exp":
                    _, H, j, h = op
                    w.append(("sMM", SCc[(H, j, h)]))
                    if exp_war[(H, j, h)] is not None:
                        w.append(exp_war[(H, j, h)])
                elif op[0] == "evac":
                    _, H, h = op
                    w.append(("sMM", AS0last[H] if h == 0 else AS1last[H]))
                elif op[0] == "ln":
                    _, H, h = op
                    w.append(("sVe", LCc[(H, h)]))
                swaits.append(w)
            vwaits = []
            chain_prev_bc = {}
            chain = [(H, h) for H in range(NH) for h in (0, 1)]
            prev_chain = {chain[i]: chain[i - 1] for i in range(1, len(chain))}
            for op in vplan:
                w = []
                if op[0] == "mask":
                    _, H, j = op
                    hd = 0 if j < 32 else 1
                    w.append(("sEv", EXc[(H, j, hd)]))
                elif op[0] == "lcopy":
                    _, H, h = op
                    w.append(("sMM", AS0last[H] if h == 0 else AS1last[H]))
                    if (H, h) in prev_chain:
                        w.append(("sMM", BCc[prev_chain[(H, h)]]))
                else:
                    _, H, h = op
                    w.append(("sMM", BCc[(H, h)]))
                    w.append(("sEv", EVAC[(H, h)]))
                vwaits.append(w)

            kv_ready = {}
            q_ready = {}

            def do_waits(eng, w):
                for kind, val in w:
                    if kind == "sMM":
                        eng.wait_ge(sMM.h, val)
                    elif kind == "sEv":
                        eng.wait_ge(sEv.h, val)
                    elif kind == "sVe":
                        eng.wait_ge(sVe.h, val)

            @blk.sync
            def _(sy):
                def kv_load(g):
                    sp = sKVa if (g - 1) % 2 == 0 else sKVb
                    for r in range(N_CORES):
                        srck = k_all.ap()[r * 1024 + g * 128:r * 1024 + (g + 1) * 128, :] \
                            .rearrange("q (m t) -> q m t", t=128)
                        dstk = KG(g).rearrange("p (m rr t) -> p m rr t", rr=8, t=128)[:, :, r, :]
                        sp.inc(sy.dma_start(out=dstk, in_=srck), 16)
                        srcv = v_all.ap()[r * 8192 + g * 1024:r * 8192 + (g + 1) * 1024, :] \
                            .rearrange("(m p) h -> p m h", p=128)
                        dstv = VG(g).rearrange("p (m rr t) -> p m rr t", rr=8, t=128)[:, :, r, :]
                        sp.inc(sy.dma_start(out=dstv, in_=srcv), 16)
                    kv_ready[g] = (sp, sp.n)

                def q_load(H):
                    sp = sQa if H % 2 == 0 else sQb
                    sp.inc(sy.dma_start(out=qt(H % 2),
                                        in_=q_dram.ap()[H * 128:(H + 1) * 128, :]), 16)
                    q_ready[H] = (sp, sp.n)

                kv_ready[0] = (sPl, sPl.n)
                q_load(0)
                q_load(1)
                kv_load(1)
                kv_load(2)
                ev_list = []
                for g in range(3, NKVH):
                    ev_list.append((AS1last[4 * (g - 1) - 1], "kv", g))
                for H in range(2, NH):
                    ev_list.append((SC1last[H - 2], "q", H))
                ev_list.sort(key=lambda x: x[0])
                for wv_, kind, idx in ev_list:
                    sy.wait_ge(sMM.h, wv_)
                    if kind == "kv":
                        kv_load(idx)
                    else:
                        q_load(idx)

            @blk.tensor
            def _(t):
                for op, w in zip(tplan, twaits):
                    do_waits(t, w)
                    if op[0] == "scores":
                        _, H, j, h, bk = op
                        if j == 0 and h <= 1 and op is not None:
                            pass
                        c0 = (j // 8) * 128
                        lo = c0 if h == 0 else max(0, c0 - 512)
                        if j == 0 and h == 0:
                            if H % 4 == 0:
                                t.wait_ge(kv_ready[H // 4][0].h, kv_ready[H // 4][1])
                            t.wait_ge(q_ready[H][0].h, q_ready[H][1])
                        kg = KG(H // 4)
                        sMM.inc(t.matmul(SCB[bk][:, lo:512],
                                         kg[:, j * 128:(j + 1) * 128],
                                         qt(H % 2)[:, h * 512 + lo:h * 512 + 512],
                                         start=True, stop=True))
                    elif op[0] == "avsum":
                        _, H, j, h, pt_ = op
                        c0 = (j // 8) * 128
                        lo = c0 if h == 0 else max(0, c0 - 512)
                        av = ps_avA if h == 0 else ps_avB
                        lb = ps_lA if h == 0 else ps_lB
                        first = (j == 0)
                        lastf = (j == 31) if h == 0 else (j == 63)
                        vg = VG(H // 4)
                        p = ptile(pt_)
                        t.matmul(av[:, lo:512], vg[:, j * 128:(j + 1) * 128],
                                 p[:, lo:512], start=first, stop=lastf)
                        sMM.inc(t.matmul(lb[:, lo:512], ones_row[:, :],
                                         p[:, lo:512], start=first, stop=lastf))
                    else:
                        _, wd, h = op
                        sMM.inc(t.matmul(ps_a1[:, :], ones_colf[:, :], lrow,
                                         start=True, stop=True))

            @blk.scalar
            def _(s):
                for op, w in zip(splan, swaits):
                    do_waits(s, w)
                    if op[0] == "exp":
                        _, H, j, h = op
                        c0 = (j // 8) * 128
                        lo = c0 if h == 0 else max(0, c0 - 512)
                        if h == 0:
                            bk = "A0" if j % 2 == 0 else "A1"
                            pt_ = j % 2
                        elif j < 32:
                            bk = "B0" if j % 2 == 0 else "B1"
                            pt_ = 2 + j % 2
                        else:
                            bk = ["B0", "B1", "A0"][(j - 32) % 3]
                            pt_ = (j - 32) % 4
                        sEv.inc(s.activation(ptile(pt_)[:, lo:512], SCB[bk][:, lo:512],
                                             AF.Exp, bias=np.float32(ESHIFT),
                                             scale=np.float32(SCALE)))
                    elif op[0] == "evac":
                        _, H, h = op
                        av = ps_avA if h == 0 else ps_avB
                        sEv.inc(s.copy(aoch(H)[:, h * 512:(h + 1) * 512], av[:, :]))
                    elif op[0] == "ln":
                        sEv.inc(s.activation(lnrow, lrow, AF.Ln))
                    else:
                        sEv.inc(s.activation(lrow, lnrow, AF.Exp, scale=np.float32(-1.0)))

            @blk.gpsimd
            def _(g):
                wo_ready.append((sWa, sWa.inc(g.dma_start(out=wobuf(0), in_=wo.ap()[0]), 16)))

            @blk.vector
            def _(v):
                for op, w in zip(vplan, vwaits):
                    do_waits(v, w)
                    if op[0] == "mask":
                        _, H, j = op
                        c0 = (j // 8) * 128
                        d = j % 8
                        if j < 32:
                            pt_ = j % 2
                            lo = c0
                        else:
                            pt_ = (j - 32) % 4
                            lo = c0 - 512
                        sVe.inc(v.tensor_tensor(ptile(pt_)[:, lo:lo + 128],
                                                ptile(pt_)[:, lo:lo + 128],
                                                mask_sb[:, d * 128:(d + 1) * 128],
                                                ALU.mult))
                    elif op[0] == "lcopy":
                        _, H, h = op
                        lb = ps_lA if h == 0 else ps_lB
                        sVe.inc(v.tensor_copy(lrow, lb[0:1, :]))
                    else:
                        _, H, h = op
                        sVe.inc(v.tensor_tensor(aoch(H)[:, h * 512:(h + 1) * 512],
                                                aoch(H)[:, h * 512:(h + 1) * 512],
                                                ps_a1[:, :], ALU.mult))

            assert sMM.n == mm_end and sEv.n == ev_end and sVe.n == ve_end, \
                (sMM.n, mm_end, sEv.n, ev_end, sVe.n, ve_end)

        if "ao" in tap_t:
            with nc.Block("tap_ao") as blk:
                @blk.gpsimd
                def _(g):
                    dst = tap_t["ao"].ap().rearrange("(c p) t -> p c t", p=128)
                    src = T1.rearrange("p (c t) -> p c t", t=1024)
                    sW.inc(g.dma_start(out=dst, in_=src), 16)
                    g.wait_ge(sW.h, sW.n)

        with (
            nc.psum_tensor("ps_o0", [128, 512], F32) as ps_o0,
            nc.psum_tensor("ps_o1", [128, 512], F32) as ps_o1,
            nc.Block("wo") as blk,
        ):
            mm0, ev0, ve0 = sMM.n, sEv.n, sVe.n
            ps_o = [ps_o0, ps_o1]
            omm = [mm0 + it + 1 for it in range(64)]
            oadd = [ve0 + it + 1 for it in range(64)]
            ka0, kb0 = sKVa.n, sKVb.n
            g_ready = {it: ((sKVa, ka0 + 16 * (it // 2 + 1)) if it % 2 == 0
                            else (sKVb, kb0 + 16 * (it // 2 + 1))) for it in range(64)}

            def gslot(it):
                return p_sb[:, (it % 2) * 512:(it % 2) * 512 + 512]

            def xslot(it):
                return T4[:, (it % 2) * 512:(it % 2) * 512 + 512]

            def oslot(it):
                return rn_sb[:, (it % 2) * 512:(it % 2) * 512 + 512]

            @blk.gpsimd
            def _(g):
                for b in (1,):
                    wo_ready.append((sWb, sWb.inc(g.dma_start(out=wobuf(b), in_=wo.ap()[b]), 16)))
                for b in range(2, 8):
                    g.wait_ge(sMM.h, omm[(b - 2) * 8 + 7])
                    sp = sWa if b % 2 == 0 else sWb
                    wo_ready.append((sp, sp.inc(g.dma_start(out=wobuf(b), in_=wo.ap()[b]), 16)))
                g.wait_ge(sW.h, sW.n)

            gx_ready = []

            @blk.scalar
            def _(s):
                for it in range(64):
                    sp = sGXa if it % 2 == 0 else sGXb
                    if it >= 2:
                        s.wait_ge(sVe.h, oadd[it - 2])
                    sp.inc(s.dma_start(out=xslot(it), in_=x_tm.ap()[it]), 16)
                    gx_ready.append((sp, sp.n))

            @blk.tensor
            def _(t):
                for b in range(8):
                    t.wait_ge(wo_ready[b][0].h, wo_ready[b][1])
                    for ti in range(8):
                        it = b * 8 + ti
                        if it >= 2:
                            t.wait_ge(sVe.h, oadd[it - 2])
                        for j in range(DCH):
                            ins = t.matmul(ps_o[it % 2][:, :],
                                           T1[:, j * 1024 + ti * 128:j * 1024 + (ti + 1) * 128],
                                           wobuf(b)[:, j, :],
                                           start=(j == 0), stop=(j == DCH - 1))
                        sMM.inc(ins)

            @blk.vector
            def _(v):
                for it in range(64):
                    v.wait_ge(sMM.h, omm[it])
                    v.wait_ge(gx_ready[it][0].h, gx_ready[it][1])
                    v.wait_ge(g_ready[it][0].h, g_ready[it][1])
                    if it >= 2:
                        v.wait_ge(sOut.h, 16 * (it - 1))
                    v.tensor_tensor(xtmp_sb[:, 0:512], ps_o[it % 2][:, :], gslot(it), ALU.mult)
                    sVe.inc(v.tensor_tensor(oslot(it), xtmp_sb[:, 0:512], xslot(it), ALU.add))

            @blk.sync
            def _(sy):
                def g_load(it):
                    sp = sKVa if it % 2 == 0 else sKVb
                    assert g_ready[it][1] == sp.inc(
                        sy.dma_start(out=gslot(it), in_=g_dram.ap()[it]), 16)

                g_load(0)
                g_load(1)
                for it in range(64):
                    sy.wait_ge(sVe.h, oadd[it])
                    sy.wait_ge(sOut.h, 16 * it)
                    sOut.inc(sy.dma_start(out=out_t.ap()[it], in_=oslot(it)), 16)
                    if it + 2 < 64:
                        g_load(it + 2)
                sy.wait_ge(sOut.h, 16 * 64)

    return nc



def _llama3_inv_freq():
    base = 500000.0
    head_dim = 128
    inv_freq = 1.0 / base ** (np.arange(0, head_dim, 2, dtype=np.float64) / head_dim)
    factor, low_f, high_f, old_ctx = 32.0, 1.0, 4.0, 8192
    low_wl, high_wl = old_ctx / low_f, old_ctx / high_f
    wavelen = 2 * math.pi / inv_freq
    inv_llama = np.where(wavelen > low_wl, inv_freq / factor, inv_freq)
    smooth = (old_ctx / wavelen - low_f) / (high_f - low_f)
    smoothed = (1 - smooth) * inv_llama / factor + smooth * inv_llama
    is_med = ~(wavelen < high_wl) & ~(wavelen > low_wl)
    return np.where(is_med, smoothed, inv_llama).astype(np.float32)


_PERM = np.empty(128, dtype=np.int64)
_PERM[0::2] = np.arange(64)
_PERM[1::2] = np.arange(64) + 64


def prep_inputs(L0_attn_x, L0_attn_Wq, L0_attn_Wk, L0_attn_Wv, L0_attn_Wo,
                L0_ffn_lora_A, L0_ffn_lora_B, L0_ffn_W, L0_ffn_bias,
                L0_ffn_lora_weights):
    c = np.ascontiguousarray
    Wq = np.asarray(L0_attn_Wq, dtype=np.float32)
    Wk = np.asarray(L0_attn_Wk, dtype=np.float32)
    Wv = np.asarray(L0_attn_Wv, dtype=np.float32)
    Wo = np.asarray(L0_attn_Wo, dtype=np.float32)
    W = np.asarray(L0_ffn_W, dtype=np.float32)
    A = np.asarray(L0_ffn_lora_A, dtype=np.float32)
    Bm = np.asarray(L0_ffn_lora_B, dtype=np.float32)
    bias = np.asarray(L0_ffn_bias, dtype=np.float32)
    lw = np.asarray(L0_ffn_lora_weights, dtype=np.float32)
    x = np.asarray(L0_attn_x, dtype=np.float32)

    Wq_p = Wq.reshape(NQH, 128, D)[:, _PERM, :].reshape(D, D)
    Wk_p = Wk.reshape(NKVH, 128, D)[:, _PERM, :].reshape(NKVH * 128, D)
    wq_h = c(Wq_p.reshape(NQH, 128, DCH, 128).transpose(0, 3, 2, 1))
    wk_h = c(Wk_p.reshape(NKVH, 128, DCH, 128).transpose(0, 3, 2, 1))
    wv_h = c(Wv.reshape(NKVH, 128, DCH, 128).transpose(0, 3, 2, 1))
    wt_h = c(W.reshape(8, 512, DCH, 128).transpose(0, 3, 2, 1)
              .reshape(8, 128, 8, 4, 512).transpose(0, 2, 1, 3, 4))
    wo_h = c(Wo.reshape(8, 512, DCH, 128).transpose(0, 3, 2, 1))
    at_h = c(A.transpose(2, 0, 1).reshape(D, 128).reshape(DCH, 128, 128).transpose(1, 0, 2))
    bt_h = c(Bm.transpose(0, 2, 1).reshape(128, D))
    bias_h = bias.reshape(1, D)
    lw_h = c(np.repeat(lw, 16).reshape(128, 1))

    inv = _llama3_inv_freq().astype(np.float64)
    xg = x.reshape(T, D)

    in_maps = []
    for i in range(N_CORES):
        xs = xg.reshape(NCH, 128, D)[i::8]
        xtm_full = xs.reshape(TPC, D)
        xtm = c(xtm_full.reshape(8, 128, 8, 512).transpose(2, 0, 1, 3).reshape(NCH, 128, 512))
        xfm = c(xtm_full.T)
        pos = np.concatenate(
            [np.arange((8 * m + i) * 128, (8 * m + i) * 128 + 128) for m in range(NSLOT)]
        ).astype(np.float64)
        ang = pos[None, :] * inv[:, None]
        cosv = np.cos(ang).astype(np.float32)
        sinv = np.sin(ang).astype(np.float32)
        cos_h = np.empty((128, TPC), dtype=np.float32)
        sin_h = np.empty((128, TPC), dtype=np.float32)
        cos_h[0::2] = cosv
        cos_h[1::2] = cosv
        sin_h[0::2] = -sinv
        sin_h[1::2] = sinv
        mask_h = np.zeros((128, NSLOT, 128), dtype=np.float32)
        kk = np.arange(128)[:, None]
        qq = np.arange(128)[None, :]
        for d in range(NSLOT):
            if d < i:
                mask_h[:, d, :] = 1.0
            elif d == i:
                mask_h[:, d, :] = (kk <= qq).astype(np.float32)
        in_maps.append({
            "x_fm": xfm, "x_tm": xtm, "wq": wq_h, "wk": wk_h, "wv": wv_h,
            "wt": wt_h, "wo": wo_h, "at": at_h, "bt": bt_h, "bias_in": bias_h,
            "lw_in": lw_h, "cos_in": cos_h, "sin_in": sin_h,
            "mask_in": c(mask_h),
        })
    return in_maps


_NC_CACHE = {}


def kernel(**inputs):
    in_maps = prep_inputs(**inputs)
    if "nc" not in _NC_CACHE:
        _NC_CACHE["nc"] = build()
    nc = _NC_CACHE["nc"]
    res = run_bass_kernel_spmd(nc, in_maps, core_ids=list(range(N_CORES)))
    full = np.empty((NCH, 128, D), dtype=np.float32)
    for i in range(N_CORES):
        o = res.results[i]["out"].reshape(8, 8, 128, 512).transpose(1, 2, 0, 3)
        full[i::8] = o.reshape(NSLOT, 128, D)
    return full.reshape(1, T, D)
